# version 36
# baseline (speedup 1.0000x reference)
import numpy as np
import ml_dtypes
import concourse.bass as bass
import concourse.mybir as mybir
from concourse.bass_utils import run_bass_kernel_spmd

F32 = mybir.dt.float32
BF16 = mybir.dt.bfloat16
AF = mybir.ActivationFunctionType
ALU = mybir.AluOpType
AX = mybir.AxisListType
BF = ml_dtypes.bfloat16

NCORES = 8
BPC = 2
S = 2048
Dm = 2048
DFF = 8192
L = 512
H = 16
HD = 128
WIN = 128
PER = 16
MEGA = 512
NSUB = 4
NMEGA = BPC * S // MEGA
MPB = S // MEGA
DQ = DFF // 4
SCALE = float(1.0 / np.sqrt(HD))
NEGINF = -1.0e30
EPS = 1e-5
INV_D = float(1.0 / Dm)


def _build_masks():
    m = np.full((3, 128, 272), NEGINF, dtype=np.float32)
    r = np.arange(128)[:, None]
    j = np.arange(256)[None, :]
    allow0 = (j >= 128) & (j <= 128 + r)
    m[0, :, :256][allow0] = 0.0
    allow1 = (j <= 15) | ((j >= r + 1) & (j <= r + 128))
    m[1, :, :256][allow1] = 0.0
    allow2 = (j >= r + 1) & (j <= r + 128)
    m[2, :, :256][allow2] = 0.0
    m[2, :, 256:] = 0.0
    return m.astype(BF)


class Prog:
    def __init__(self, nc):
        self.nc = nc
        self.ops = {k: [] for k in ("tensor", "vector", "scalar", "sync", "gpsimd")}
        self.sems = {}
        self.c = {}

    def sem(self, name):
        self.sems[name] = self.nc.alloc_semaphore(name)
        self.c[name] = 0

    def wait(self, eng, name, val):
        if val <= 0:
            return
        assert val <= self.c[name], (eng, name, val, self.c[name])
        h = self.sems[name]
        self.ops[eng].append(lambda e, h=h, v=val: e.wait_ge(h, v))

    def waitt(self, eng, ms):
        if ms is not None:
            self.wait(eng, ms[0], ms[1])

    def chain(self, eng, sem, fn):
        self.wait(eng, sem, self.c[sem])
        return self.do(eng, fn, inc=sem)

    def do(self, eng, fn, inc=None, n=1):
        if inc is None:
            self.ops[eng].append(fn)
            return None
        h = self.sems[inc]
        self.c[inc] += n
        self.ops[eng].append(lambda e, fn=fn, h=h, n=n: fn(e).then_inc(h, n))
        return self.c[inc]

    def dma(self, eng, out, in_, sname):
        return self.do(eng, lambda e, o=out, i=in_: e.dma_start(out=o, in_=i),
                       inc=sname, n=16)


class Ring:
    def __init__(self, p, name, bufs):
        self.p = p
        self.name = name
        self.bufs = bufs
        self.n = len(bufs)
        self.i = 0
        self.free = [None] * self.n
        self.sems = []
        for j in range(self.n):
            sn = "%s%d" % (name, j)
            p.sem(sn)
            self.sems.append(sn)
        self.loaded = [(self.sems[j], 0) for j in range(self.n)]

    def next(self):
        j = self.i % self.n
        self.i += 1
        return j


def build(nmega=NMEGA, phases="LAF"):
    nc = bass.Bass("TRN2", target_bir_lowering=False)
    p = Prog(nc)
    for s in ("s_T", "s_V", "s_S",
              "c_const", "c_sm", "c_b1",
              "lnT",
              "xs", "xi1", "xi2", "xi3",
              "o0", "o1", "o2", "o3"):
        p.sem(s)

    x_d = nc.dram_tensor("x", [BPC, S, Dm], F32, kind="ExternalInput").ap()
    wq_d = nc.dram_tensor("wq", [Dm, Dm], F32, kind="ExternalInput").ap()
    wkv_d = nc.dram_tensor("wkv", [Dm, L], F32, kind="ExternalInput").ap()
    wk_d = nc.dram_tensor("wk", [L, Dm], F32, kind="ExternalInput").ap()
    wv_d = nc.dram_tensor("wv", [L, Dm], F32, kind="ExternalInput").ap()
    wo_d = nc.dram_tensor("wo", [Dm, Dm], F32, kind="ExternalInput").ap()
    rw_d = nc.dram_tensor("rw", [Dm], F32, kind="ExternalInput").ap()
    rb_d = nc.dram_tensor("rb", [1], F32, kind="ExternalInput").ap()
    w1_d = nc.dram_tensor("w1", [Dm, DFF], F32, kind="ExternalInput").ap()
    b1_d = nc.dram_tensor("b1", [DFF], F32, kind="ExternalInput").ap()
    w2_d = nc.dram_tensor("w2", [DFF, Dm], F32, kind="ExternalInput").ap()
    b2_d = nc.dram_tensor("b2", [Dm], F32, kind="ExternalInput").ap()
    y_d = nc.dram_tensor("y", [BPC, S, Dm], F32, kind="ExternalOutput").ap()

    wkv_t = nc.dram_tensor("wkv_t", [4, 128, 16, 128], BF16).ap()
    wq_t = nc.dram_tensor("wq_t", [16, 128, 16, 128], BF16).ap()
    wk_t = nc.dram_tensor("wk_t", [16, 128, 4, 128], BF16).ap()
    wv_t = nc.dram_tensor("wv_t", [8, 128, 4, 256], BF16).ap()
    wo_t = nc.dram_tensor("wo_t", [8, 128, 16, 256], BF16).ap()
    w1_t = nc.dram_tensor("w1_t", [64, 128, 16, 128], BF16).ap()
    w2_t = nc.dram_tensor("w2_t", [4, 8, 128, 16, 256], BF16).ap()

    id_bf_d = nc.inline_tensor(np.eye(128, dtype=BF), "id_bf").ap()
    id_f32_d = nc.inline_tensor(np.eye(128, dtype=np.float32), "id_f32").ap()
    masks_d = nc.inline_tensor(_build_masks(), "masks").ap()

    _BASE = 16512

    def A(name, shape, dtype, offset):
        return nc.alloc_sbuf_tensor_at(name, shape, dtype, offset=_BASE + offset)
    acc = A("acc", [128, NSUB, Dm], F32, offset=0)
    xnT = A("xnT", [128, 16, MEGA], BF16, offset=32768)
    kbuf = A("kbuf", [128, 16, 640], BF16, offset=49152)
    vbuf = A("vbuf", [128, 5, Dm], BF16, offset=69632)
    qT = A("qT", [128, 16, MEGA], BF16, offset=90112)
    hT = A("hT", [128, 16, MEGA], BF16, offset=90112)
    aoT = A("aoT", [128, 16, MEGA], BF16, offset=106496)
    xside = A("xside", [128, Dm], F32, offset=122880)
    p_sb = [A("p_sb%d" % j, [128, 272], BF16, offset=122880 + 576 * j)
            for j in range(4)]
    pT_sb = [A("pT_sb0", [128, 384], BF16, offset=125184),
             A("pT_sb1", [128, 384], BF16, offset=125952)]
    kvlatT = A("kvlatT", [128, 4, MEGA], BF16, offset=126720)
    wkv_s = [A("wkv_s0", [128, 16, 128], BF16, offset=131072),
             A("wkv_s1", [128, 16, 128], BF16, offset=135168)]
    wq_s = [A("wq_s0", [128, 16, 128], BF16, offset=139264),
            A("wq_s1", [128, 16, 128], BF16, offset=143360)]
    wk_s = [A("wk_s0", [128, 4, 128], BF16, offset=147456),
            A("wk_s1", [128, 4, 128], BF16, offset=148480)]
    wv_s = [A("wv_s0", [128, 4, 256], BF16, offset=149504),
            A("wv_s1", [128, 4, 256], BF16, offset=151552)]
    wo_s = [A("wo_s0", [128, 16, 256], BF16, offset=153600),
            A("wo_s1", [128, 16, 256], BF16, offset=161792)]
    w1_s = [A("w1_s0", [128, 16, 128], BF16, offset=169984),
            A("w1_s1", [128, 16, 128], BF16, offset=174080),
            A("w1_s2", [128, 16, 128], BF16, offset=194560)]
    w2_s = [A("w2_s0", [128, 16, 256], BF16, offset=178176),
            A("w2_s1", [128, 16, 256], BF16, offset=186368)]
    stg_bf = A("stg_bf", [128, 128], BF16, offset=194560)
    stg_f32 = A("stg_f32", [128, 128], F32, offset=194560)
    vper = A("vper", [16, Dm], BF16, offset=198656)
    kper = A("kper", [128, 16, 16], BF16, offset=202752)
    masks_sb = A("masks_sb", [128, 3, 272], BF16, offset=203264)
    id_bf = A("id_bf_sb", [128, 128], BF16, offset=204928)
    id_f32 = A("id_f32_sb", [128, 128], F32, offset=205184)
    ones_row = A("ones_row", [1, 128], BF16, offset=205696)
    b1T = A("b1T", [128, 64], F32, offset=205952)
    rwT = A("rwT", [128, 16], BF16, offset=206208)
    b2bc = A("b2bc", [128, Dm], BF16, offset=206272)
    rb_sb = A("rb_sb", [1, 1], BF16, offset=210368)
    rbh = A("rbh", [128, 1], F32, offset=210432)
    sumx = A("sumx", [128, NSUB], F32, offset=210496)
    sumsq = A("sumsq", [128, NSUB], F32, offset=210560)
    meanv = A("meanv", [128, NSUB], F32, offset=210624)
    varv = A("varv", [128, NSUB], F32, offset=210688)
    stdv = A("stdv", [128, NSUB], F32, offset=210752)
    rstd = A("rstd", [128, NSUB], F32, offset=210816)
    s0c = A("s0c", [128, NSUB], F32, offset=210880)
    s1c = A("s1c", [128, NSUB], F32, offset=210944)
    tstat = A("tstat", [128, NSUB], F32, offset=211008)
    gcol = A("gcol", [128, NSUB], F32, offset=211072)
    negmax = [A("negmax%d" % j, [128, 1], F32, offset=211136 + 64 * j)
              for j in range(4)]
    rsum = [A("rsum%d" % j, [128, 1], F32, offset=211392 + 64 * j)
            for j in range(4)]
    rinv = [A("rinv%d" % j, [128, 1], F32, offset=211648 + 64 * j)
            for j in range(4)]

    ps_ev = nc.alloc_psum_tensor("ps_ev", [128, 512], F32)
    ps_ev2 = nc.alloc_psum_tensor("ps_ev2", [128, 512], F32)
    ps_s = [nc.alloc_psum_tensor("ps_sA", [128, 512], F32),
            nc.alloc_psum_tensor("ps_sB", [128, 512], F32)]
    ps_tr2 = [nc.alloc_psum_tensor("ps_trA", [128, 1024], BF16),
              nc.alloc_psum_tensor("ps_trB", [128, 1024], BF16)]
    ps_o2 = [nc.alloc_psum_tensor("ps_oA", [128, 512], F32),
             nc.alloc_psum_tensor("ps_oB", [128, 512], F32)]

    ev_ring = Ring(p, "ev", [ps_ev, ps_ev2, ps_s[0], ps_s[1]])
    ps_s4 = ev_ring.bufs
    score_free = [None, None, None, None]

    p.dma("sync", id_bf[:], id_bf_d, "c_const")
    p.dma("sync", id_f32[:], id_f32_d, "c_const")
    for i in range(3):
        p.dma("sync", masks_sb[:, i, :], masks_d[i], "c_const")
    c_const = p.c["c_const"]
    p.do("vector", lambda e: e.memset(ones_row[:], 1.0), inc="s_V")
    p.do("vector", lambda e: e.memset(stg_bf[:], 0.0), inc="s_V")
    v_stg0 = p.c["s_V"]
    p.wait("tensor", "s_V", v_stg0)
    p.wait("tensor", "c_const", c_const)
    p.do("tensor", lambda e: e.transpose(ps_tr2[0][:, 256:384], stg_bf[:],
                                         id_bf[:]))
    p.do("tensor", lambda e: e.transpose(ps_tr2[1][:, 256:384], stg_bf[:],
                                         id_bf[:]), inc="s_T")
    t_trz = p.c["s_T"]
    p.wait("gpsimd", "s_V", v_stg0)
    p.wait("gpsimd", "s_T", t_trz)
    p.dma("gpsimd", stg_bf[0:16, :], rw_d.rearrange("(p c) -> p c", c=128), "c_sm")
    p.dma("gpsimd", rb_sb[:], rb_d[None, :], "c_sm")
    b2stage = A("b2stage", [1, Dm], BF16, offset=90112)
    p.dma("gpsimd", b2stage[:], b2_d[None, :], "c_sm")
    c_smalls = p.c["c_sm"]
    cast_at = {}

    def cast_class(base, dst, src, n, grp=4):
        lst = []
        for g0 in range(0, n, grp):
            sn = "%s_g%d" % (base, g0 // grp)
            p.sem(sn)
            for j in range(g0, min(g0 + grp, n)):
                p.dma("gpsimd", dst[j], src[j], sn)
            tot = p.c[sn]
            for j in range(g0, min(g0 + grp, n)):
                lst.append((sn, tot))
        return lst

    wkvr = wkv_d.rearrange("(c p) (j m) -> j p c m", p=128, m=128)
    cast_at["wkv"] = cast_class("c_wkv", wkv_t, wkvr, 4)
    wqr = wq_d.rearrange("(c p) (j m) -> j p c m", p=128, m=128)
    cast_at["wq"] = cast_class("c_wq", wq_t, wqr, 16)
    wkr = wk_d.rearrange("(c p) (j m) -> j p c m", p=128, m=128)
    cast_at["wk"] = cast_class("c_wk", wk_t, wkr, 16)
    wvr = wv_d.rearrange("(c p) (s n) -> s p c n", p=128, n=256)
    cast_at["wv"] = cast_class("c_wv", wv_t, wvr, 8)
    w1r = w1_d.rearrange("(c p) (j m) -> j p c m", p=128, m=128)
    w2r = w2_d.rearrange("(q c p) (s n) -> q s p c n", q=4, p=128, n=256)
    wor = wo_d.rearrange("(c p) (s n) -> s p c n", p=128, n=256)

    def cast_w1(jq):
        return cast_class("c_w1_%d" % jq, w1_t[jq * 16:(jq + 1) * 16],
                          w1r[jq * 16:(jq + 1) * 16], 16)

    def cast_w2(q):
        return cast_class("c_w2_%d" % q, w2_t[q], w2r[q], 8)

    cast_at["w1"] = [None] * 4
    cast_at["w2"] = [None] * 4
    cast_at["w1"][0] = cast_w1(0)
    cast_at["wo"] = cast_class("c_wo", wo_t, wor, 8)
    cast_at["w2"][0] = cast_w2(0)
    for q in range(1, 4):
        cast_at["w1"][q] = cast_w1(q)
        cast_at["w2"][q] = cast_w2(q)

    p.wait("tensor", "c_sm", c_smalls)
    p.wait("tensor", "c_const", c_const)
    p.wait("tensor", "s_V", v_stg0)
    p.do("tensor", lambda e: e.transpose(ps_tr2[0][:, 0:128], stg_bf[:], id_bf[:]),
         inc="s_T")
    t_rw = p.c["s_T"]
    p.wait("vector", "s_T", t_rw)
    p.do("vector", lambda e: e.tensor_copy(rwT[:], ps_tr2[0][:, 0:16]), inc="s_V")
    v_rwT = p.c["s_V"]
    p.wait("vector", "s_T", t_rw)
    p.do("vector", lambda e: e.memset(stg_f32[:], 0.0), inc="s_V")
    v_stgf = p.c["s_V"]
    p.wait("sync", "s_V", v_stgf)
    p.dma("sync", stg_f32[0:64, :], b1_d.rearrange("(p c) -> p c", c=128), "c_b1")
    c_b1 = p.c["c_b1"]
    p.wait("tensor", "c_b1", c_b1)
    p.wait("tensor", "s_V", v_stgf)
    p.do("tensor", lambda e: e.transpose(ps_o2[0][:, 0:128], stg_f32[:], id_f32[:]),
         inc="s_T")
    t_b1 = p.c["s_T"]
    p.wait("vector", "s_T", t_b1)
    p.do("vector", lambda e: e.tensor_copy(b1T[:], ps_o2[0][:, 0:64]), inc="s_V")
    v_b1T = p.c["s_V"]
    p.wait("tensor", "s_V", v_stg0)
    p.wait("tensor", "s_V", v_b1T)
    p.do("tensor", lambda e: e.matmul(ps_o2[0][:, 128:129], ones_row[:], rb_sb[:],
                                      start=True, stop=True), inc="s_T")
    t_rb = p.c["s_T"]
    p.wait("vector", "s_T", t_rb)
    p.do("vector", lambda e: e.tensor_scalar(rbh[:], ps_o2[0][:, 128:129], 0.5,
                                             None, ALU.mult), inc="s_V")
    v_rbh = p.c["s_V"]
    v_prev = v_rbh
    for j in range(4):
        p.wait("tensor", "s_V", v_prev)
        p.do("tensor", lambda e, j=j: e.matmul(
            ps_o2[j % 2][:, 0:512], ones_row[:],
            b2stage[0:1, j * 512:(j + 1) * 512], start=True, stop=True),
            inc="s_T")
        t_j = p.c["s_T"]
        p.wait("vector", "s_T", t_j)
        p.chain("vector", "s_V", lambda e, j=j: e.tensor_copy(
            b2bc[:, j * 512:(j + 1) * 512], ps_o2[j % 2][:, 0:512]))
        v_prev = p.c["s_V"]
    t_b2bc = p.c["s_T"]
    v_b2bc = p.c["s_V"]
    ao_free = [("s_V", v_b2bc), ("s_V", v_b2bc)]
    tr_free = [("s_V", v_rwT), None]


    p.dma("scalar", xside[:], x_d[0, 0:128, :], "xs")
    xin_sub = [None] * NSUB
    xin_sub[0] = ("xs", p.c["xs"])
    for s in range(1, NSUB):
        sn = "xi%d" % s
        p.dma("scalar", acc[:, s, :], x_d[0, s * 128:(s + 1) * 128, :], sn)
        xin_sub[s] = (sn, p.c[sn])

    R_wkv = Ring(p, "wkv", wkv_s)
    R_wq = Ring(p, "wq", wq_s)
    R_wk = Ring(p, "wk", wk_s)
    R_wv = Ring(p, "wv", wv_s)
    R_wo = Ring(p, "wo", wo_s)
    R_w1 = Ring(p, "w1", w1_s)
    R_w1.free[2] = ("s_T", t_b1)
    R_w2 = Ring(p, "w2", w2_s)

    def ring_load(ring, src_ap, cast_need, queues=("sync",)):
        j = ring.next()
        qe = queues[(ring.i - 1) % len(queues)]
        if ring.free[j] is not None:
            p.wait(qe, ring.free[j][0], ring.free[j][1])
        p.waitt(qe, cast_need)
        cnt = p.dma(qe, ring.bufs[j][:], src_ap, ring.sems[j])
        ring.loaded[j] = (ring.sems[j], cnt)
        return j

    def ev_next():
        j = ev_ring.next()
        if ev_ring.free[j] is not None:
            p.wait("tensor", ev_ring.free[j][0], ev_ring.free[j][1])
        return j

    xnT_last_reader_T = 0
    hT_last_reader_T = t_b2bc
    core_last_T = 0
    acc_out_at = [None] * NSUB
    xside_consumed_V = 0
    aoT_last_reader_T = 0
    gcol_read_V = 0
    ln_V_done = 0
    acc_base_S = None
    ln_carry = None
    xin_side_next = None
    g1_prev = [None, None]
    proj_last_T = 0
    kv_written_V = 0

    for m in range(nmega):
        b = m // MPB
        mm = m % MPB
        row0 = mm * MEGA

        xnb = [aoT[:, 0:4, :]] + [hT[:, 4 * s:4 * (s + 1), :]
                                  for s in range(1, NSUB)]
        red_V = [0] * NSUB
        sq_S = [0] * NSUB
        mathA_V = [0] * NSUB
        std_S = [0] * NSUB
        mathB_V = [0] * NSUB
        xn_S = [0] * NSUB
        fold_S = [0] * NSUB
        hb = [0]

        def heartbeat():
            p.wait("tensor", "s_V", p.c["s_V"])
            p.do("tensor", lambda e: e.matmul(
                ps_o2[1][:, 264:265], ones_row[:], ones_row[0:1, 0:1],
                start=True, stop=True))

        def ln_stage_a(u):
            xa = (xside[:] if u == 0 else acc[:, u, :])
            p.waitt("vector", xin_sub[u])
            p.chain("vector", "s_V",
                    lambda e, xa=xa, u=u: e.reduce_sum(sumx[:, u:u + 1], xa,
                                                       axis=AX.X))
            red_V[u] = p.c["s_V"]
            p.waitt("scalar", xin_sub[u])
            if u == 0:
                p.wait("scalar", "s_T", proj_last_T)
                p.wait("scalar", "s_V", ln_V_done)
            if u == 1:
                p.wait("scalar", "s_T", hT_last_reader_T)
            p.do("scalar", lambda e, xa=xa, u=u: e.activation(
                out=xnb[u][:], in_=xa, func=AF.Square,
                accum_out=sumsq[:, u:u + 1]), inc="s_S")
            sq_S[u] = p.c["s_S"]

        def ln_stage_b(u):
            p.wait("vector", "s_S", sq_S[u])
            p.chain("vector", "s_V",
                    lambda e, u=u: e.tensor_scalar(
                        meanv[:, u:u + 1], sumx[:, u:u + 1], INV_D, None,
                        ALU.mult))
            p.chain("vector", "s_V",
                    lambda e, u=u: e.tensor_scalar(
                        varv[:, u:u + 1], sumsq[:, u:u + 1], INV_D, EPS,
                        ALU.mult, ALU.add))
            p.chain("vector", "s_V",
                    lambda e, u=u: e.tensor_tensor(
                        s1c[:, u:u + 1], meanv[:, u:u + 1], meanv[:, u:u + 1],
                        ALU.mult))
            p.chain("vector", "s_V",
                    lambda e, u=u: e.tensor_tensor(
                        varv[:, u:u + 1], varv[:, u:u + 1], s1c[:, u:u + 1],
                        ALU.subtract))
            mathA_V[u] = p.c["s_V"]
            p.wait("scalar", "s_V", mathA_V[u])
            p.do("scalar", lambda e, u=u: e.activation(
                out=stdv[:, u:u + 1], in_=varv[:, u:u + 1], func=AF.Sqrt),
                inc="s_S")
            std_S[u] = p.c["s_S"]

        def ln_stage_c(u, skip_xn=False):
            xa = (xside[:] if u == 0 else acc[:, u, :])
            if not skip_xn:
                p.wait("vector", "s_S", std_S[u])
                p.chain("vector", "s_V",
                        lambda e, u=u: e.reciprocal(rstd[:, u:u + 1],
                                                    stdv[:, u:u + 1]))
                p.chain("vector", "s_V",
                        lambda e, u=u: e.tensor_scalar(
                            s0c[:, u:u + 1], rstd[:, u:u + 1], 1.0, None,
                            ALU.add))
                p.chain("vector", "s_V",
                        lambda e, u=u: e.tensor_tensor(
                            s1c[:, u:u + 1], meanv[:, u:u + 1],
                            rstd[:, u:u + 1], ALU.mult))
                p.chain("vector", "s_V",
                        lambda e, u=u: e.tensor_scalar(
                            s1c[:, u:u + 1], s1c[:, u:u + 1], -1.0, None,
                            ALU.mult))
                mathB_V[u] = p.c["s_V"]
                p.chain("vector", "s_V",
                        lambda e, xa=xa, u=u: e.tensor_scalar(
                            xnb[u][:], xa, rstd[:, u:u + 1], s1c[:, u:u + 1],
                            ALU.mult, ALU.add))
                xn_S[u] = ("s_V", p.c["s_V"])
            p.waitt("scalar", xn_S[u])
            if u == 0:
                p.waitt("scalar", acc_out_at[0])
            p.do("scalar", lambda e, xa=xa, u=u: e.activation(
                out=acc[:, u, :], in_=xa, func=AF.Identity,
                scale=s0c[:, u:u + 1], bias=s1c[:, u:u + 1]), inc="s_S")
            fold_S[u] = p.c["s_S"]

        def ln_stage_d(u):
            p.waitt("tensor", xn_S[u])
            for c in range(16):
                tb = c % 2
                p.waitt("tensor", tr_free[tb])
                p.do("tensor", lambda e, u=u, c=c, tb=tb: e.transpose(
                    ps_tr2[tb][:, 0:128],
                    xnb[u][:, c // 4, (c % 4) * 128:(c % 4 + 1) * 128],
                    id_bf[:]), inc="s_T")
                tr_T = p.c["s_T"]
                p.wait("vector", "s_T", tr_T)
                if u == 0 and c == 0:
                    p.wait("vector", "s_T", xnT_last_reader_T)
                p.do("vector", lambda e, u=u, c=c, tb=tb: e.tensor_copy(
                    xnT[:, c, u * 128:(u + 1) * 128],
                    ps_tr2[tb][:, 0:128]), inc="s_V")
                tr_free[tb] = ("s_V", p.c["s_V"])
                if c % 8 == 7:
                    heartbeat()

        if ln_carry is not None:
            (red_V[0], sq_S[0], mathA_V[0], std_S[0], mathB_V[0],
             xn_S[0]) = ln_carry
        for u in range(NSUB + 2):
            if u < NSUB and not (u == 0 and ln_carry is not None):
                ln_stage_a(u)
            if 1 <= u <= NSUB and not (u == 1 and ln_carry is not None):
                ln_stage_b(u - 1)
            if u >= 2:
                ln_stage_c(u - 2, skip_xn=(u == 2 and ln_carry is not None))
                ln_stage_d(u - 2)
        ln_carry = None
        xside_consumed_V = red_V[0]
        acc_base_S = ("s_S", fold_S[NSUB - 1])
        ln_V_done = p.c["s_V"]
        lnT_done = ("s_V", p.c["s_V"])
        lnT_prev = lnT_done

        p.wait("vector", "s_T", core_last_T)
        p.wait("vector", "s_V", kv_written_V)
        if mm == 0:
            p.do("vector", lambda e: e.memset(kbuf[:, :, 0:128], 0.0), inc="s_V")
            p.do("vector", lambda e: e.memset(vbuf[:, 0, :], 0.0), inc="s_V")
        else:
            p.do("vector", lambda e: e.tensor_copy(kbuf[:, :, 0:128],
                                                   kbuf[:, :, 512:640]), inc="s_V")
            p.do("vector", lambda e: e.tensor_copy(vbuf[:, 0, :], vbuf[:, 4, :]),
                 inc="s_V")
        roll_V = p.c["s_V"]

        p.waitt("tensor", ao_free[0])
        p.waitt("tensor", lnT_done)
        p.wait("tensor", "s_V", v_rwT)
        for s in range(NSUB):
            for c in range(16):
                p.do("tensor", lambda e, s=s, c=c: e.matmul(
                    ps_o2[0][:, s:s + 1], xnT[:, c, s * 128:(s + 1) * 128],
                    rwT[:, c:c + 1], start=(c == 0), stop=(c == 15)),
                    inc=("s_T" if c == 15 else None))
        gate_T = p.c["s_T"]
        p.wait("scalar", "s_T", gate_T)
        p.wait("scalar", "s_V", max(v_rbh, gcol_read_V))
        p.do("scalar", lambda e: e.activation(out=tstat[:], in_=ps_o2[0][:, 0:NSUB],
                                              func=AF.Tanh, bias=rbh[:], scale=0.5),
             inc="s_S")
        gate_S = p.c["s_S"]
        p.wait("vector", "s_S", gate_S)
        p.wait("vector", "s_V", gcol_read_V)
        p.do("vector", lambda e: e.tensor_scalar(gcol[:], tstat[:], 0.5, 0.5,
                                                 ALU.mult, ALU.add), inc="s_V")
        gcol_V = p.c["s_V"]
        ao_free[0] = ("s_S", gate_S)

        qs2 = ("sync", "scalar") if m > 0 else ("sync",)
        ins_on = m > 0

        kvlat_V = []
        for j in range(4):
            jj = ring_load(R_wkv, wkv_t[j], cast_at["wkv"][j], queues=qs2)
            bk = ev_next()
            p.waitt("tensor", R_wkv.loaded[jj])
            p.waitt("tensor", lnT_done)
            for c in range(16):
                p.do("tensor", lambda e, jj=jj, c=c, bk=bk: e.matmul(
                    ev_ring.bufs[bk][:], R_wkv.bufs[jj][:, c, :], xnT[:, c, :],
                    start=(c == 0), stop=(c == 15)),
                    inc=("s_T" if c == 15 else None))
            g_T = p.c["s_T"]
            R_wkv.free[jj] = ("s_T", g_T)
            p.wait("vector", "s_T", g_T)
            p.do("vector", lambda e, j=j, bk=bk: e.tensor_copy(
                kvlatT[:, j, :], ev_ring.bufs[bk][:]), inc="s_V")
            kvlat_V.append(p.c["s_V"])
            ev_ring.free[bk] = ("s_V", p.c["s_V"])
        kvlat_ready_V = p.c["s_V"]

        for j in range(16):
            jj = ring_load(R_wq, wq_t[j], cast_at["wq"][j], queues=qs2)
            bk = ev_next()
            p.waitt("tensor", R_wq.loaded[jj])
            if j == 0:
                p.waitt("tensor", lnT_done)
            for c in range(16):
                p.do("tensor", lambda e, jj=jj, c=c, bk=bk: e.matmul(
                    ev_ring.bufs[bk][:], R_wq.bufs[jj][:, c, :], xnT[:, c, :],
                    start=(c == 0), stop=(c == 15)),
                    inc=("s_T" if c == 15 else None))
            g_T = p.c["s_T"]
            R_wq.free[jj] = ("s_T", g_T)
            p.wait("vector", "s_T", g_T)
            if j == 0:
                p.wait("vector", "s_T", hT_last_reader_T)
            p.do("vector", lambda e, j=j, bk=bk: e.tensor_scalar(
                qT[:, j, :], ev_ring.bufs[bk][:], SCALE, None, ALU.mult),
                inc="s_V")
            ev_ring.free[bk] = ("s_V", p.c["s_V"])
        qT_ready_V = p.c["s_V"]

        for j in range(16):
            jj = ring_load(R_wk, wk_t[j], cast_at["wk"][j], queues=qs2)
            bk = ev_next()
            p.waitt("tensor", R_wk.loaded[jj])
            if j == 0:
                p.wait("tensor", "s_V", kvlat_ready_V)
            for c in range(4):
                p.do("tensor", lambda e, jj=jj, c=c, bk=bk: e.matmul(
                    ev_ring.bufs[bk][:], R_wk.bufs[jj][:, c, :], kvlatT[:, c, :],
                    start=(c == 0), stop=(c == 3)),
                    inc=("s_T" if c == 3 else None))
            g_T = p.c["s_T"]
            R_wk.free[jj] = ("s_T", g_T)
            p.wait("vector", "s_T", g_T)
            if j == 0:
                p.wait("vector", "s_V", roll_V)
            p.do("vector", lambda e, j=j, bk=bk: e.tensor_copy(
                kbuf[:, j, 128:640], ev_ring.bufs[bk][:]), inc="s_V")
            ev_ring.free[bk] = ("s_V", p.c["s_V"])
        k_ready_V = p.c["s_V"]

        for j in range(8):
            jj = ring_load(R_wv, wv_t[j], cast_at["wv"][j], queues=qs2)
            p.waitt("tensor", R_wv.loaded[jj])
            for s in range(NSUB):
                bk = ev_next()
                for c in range(4):
                    p.do("tensor", lambda e, jj=jj, c=c, s=s, bk=bk: e.matmul(
                        ev_ring.bufs[bk][:, 0:256],
                        kvlatT[:, c, s * 128:(s + 1) * 128],
                        R_wv.bufs[jj][:, c, :],
                        start=(c == 0), stop=(c == 3)),
                        inc=("s_T" if c == 3 else None))
                g_T = p.c["s_T"]
                p.wait("vector", "s_T", g_T)
                if j == 0 and s == 0:
                    p.wait("vector", "s_V", roll_V)
                p.do("vector", lambda e, j=j, s=s, bk=bk: e.tensor_copy(
                    vbuf[:, s + 1, j * 256:(j + 1) * 256],
                    ev_ring.bufs[bk][:, 0:256]), inc="s_V")
                ev_ring.free[bk] = ("s_V", p.c["s_V"])
            R_wv.free[jj] = ("s_T", p.c["s_T"])
        v_ready_V = p.c["s_V"]

        w1q_slot = {}
        if ins_on:
            for b3 in range(3):
                w1q_slot[b3] = ring_load(R_w1, w1_t[b3], cast_at["w1"][0][b3],
                                         queues=("scalar",))

        if mm == 0:
            p.chain("vector", "s_V",
                    lambda e: e.tensor_copy(kper[:], kbuf[:, :, 128:144]))
            p.chain("vector", "s_V",
                    lambda e: e.tensor_copy(vper[:], vbuf[0:16, 1, :]))
        per_V = p.c["s_V"]
        kv_written_V = p.c["s_V"]

        NIT = NSUB * H
        NBLK = 8
        ins_sched = [[] for _ in range(NIT + 5)]
        if ins_on:
            pairs = [(b, kk) for b in range(NBLK) for kk in range(16)]
            li = 0
            for l in range(64):
                for _ in range(2):
                    ins_sched[l].append(pairs[li])
                    li += 1
            assert li == NBLK * 16

        st_T = [0] * NIT
        ma_V = [0] * NIT
        ex_S = [0] * NIT
        ts_V = [0] * NIT
        tr_T = [0] * NIT
        pe_S = [0] * NIT
        av_T = [0] * NIT
        ao_V = [None] * NIT

        def it_sh(i):
            return i % NSUB, i // NSUB

        def emit_scores(i):
            s, h = it_sh(i)
            b3 = i % 3
            if i >= 3:
                p.wait("tensor", "s_S", ex_S[i - 3])
            else:
                p.waitt("tensor", score_free[b3])
                if i == 0:
                    p.wait("tensor", "s_V", max(k_ready_V, v_ready_V, per_V,
                                                qT_ready_V))
            p.do("tensor", lambda e, s=s, h=h, b3=b3: e.matmul(
                ps_s4[b3][:, 0:256], qT[:, h, s * 128:(s + 1) * 128],
                kbuf[:, h, s * 128:s * 128 + 256], start=True, stop=True))
            p.do("tensor", lambda e, s=s, h=h, b3=b3: e.matmul(
                ps_s4[b3][:, 256:272], qT[:, h, s * 128:(s + 1) * 128],
                kper[:, h, :], start=True, stop=True), inc="s_T")
            st_T[i] = p.c["s_T"]

        def emit_front_v(i):
            s, h = it_sh(i)
            t = mm * 4 + s
            mid = 0 if t == 0 else (1 if t == 1 else 2)
            b3 = i % 3
            b4 = i % 4
            p.wait("vector", "s_T", st_T[i])
            if m == 0 and i == 0:
                p.wait("vector", "c_const", c_const)
            p.chain("vector", "s_V",
                    lambda e, b3=b3, mid=mid: e.tensor_tensor(
                        ps_s4[b3][:, 0:272], ps_s4[b3][:, 0:272],
                        masks_sb[:, mid, :], ALU.add))
            p.chain("vector", "s_V",
                    lambda e, b3=b3, b4=b4: e.reduce_max(
                        negmax[b4][:], ps_s4[b3][:, 0:272], axis=AX.X,
                        negate=True))
            ma_V[i] = p.c["s_V"]

        def emit_exp(i):
            b3 = i % 3
            b4 = i % 4
            p.wait("scalar", "s_V", ma_V[i])
            if i >= 4:
                p.wait("scalar", "s_V", ts_V[i - 4])
            p.do("scalar", lambda e, b3=b3, b4=b4: e.activation(
                out=p_sb[b4][:], in_=ps_s4[b3][:, 0:272], func=AF.Exp,
                bias=negmax[b4][:], accum_out=rsum[b4][:]), inc="s_S")
            ex_S[i] = p.c["s_S"]

        def emit_softmax_norm(i):
            b4 = i % 4
            p.wait("vector", "s_S", ex_S[i])
            if i >= 4:
                p.wait("vector", "s_T", tr_T[i - 4])
            p.chain("vector", "s_V",
                    lambda e, b4=b4: e.reciprocal(rinv[b4][:], rsum[b4][:]))
            p.chain("vector", "s_V",
                    lambda e, b4=b4: e.tensor_scalar(
                        p_sb[b4][:], p_sb[b4][:], rinv[b4][:], None, ALU.mult))
            ts_V[i] = p.c["s_V"]

        def emit_back(i):
            s, h = it_sh(i)
            par = i % 2
            b4 = i % 4
            p.wait("tensor", "s_V", ts_V[i])
            if i >= 2:
                p.wait("tensor", "s_S", pe_S[i - 2])
            else:
                p.waitt("tensor", tr_free[par])
            p.do("tensor", lambda e, par=par, b4=b4: e.transpose(
                ps_tr2[par][:, 0:128], p_sb[b4][:, 0:128], id_bf[:]))
            p.do("tensor", lambda e, par=par, b4=b4: e.transpose(
                ps_tr2[par][:, 128:256], p_sb[b4][:, 128:256], id_bf[:]))
            p.do("tensor", lambda e, par=par, b4=b4: e.transpose(
                ps_tr2[par][0:16, 256:384], p_sb[b4][:, 256:272], id_bf[:]),
                inc="s_T")
            tr_T[i] = p.c["s_T"]
            p.wait("scalar", "s_T", tr_T[i])
            if i >= 2:
                p.wait("scalar", "s_T", av_T[i - 2])
            p.do("scalar", lambda e, par=par: e.copy(pT_sb[par][:, 0:384],
                                                     ps_tr2[par][:, 0:384]),
                 inc="s_S")
            pe_S[i] = p.c["s_S"]

        av_banks = [ps_o2[0], ev_ring.bufs[3]]

        def emit_av(i):
            s, h = it_sh(i)
            ab = i % 2
            p.wait("tensor", "s_S", pe_S[i])
            if i >= 2:
                p.waitt("tensor", ao_V[i - 2])
            elif i == 0:
                p.waitt("tensor", ao_free[0])
                p.wait("tensor", "s_S", gate_S)
            else:
                p.waitt("tensor", ev_ring.free[3])
            p.do("tensor", lambda e, s=s, h=h, ab=ab: e.matmul(
                av_banks[ab][:, 0:128], vbuf[:, s, h * 128:(h + 1) * 128],
                pT_sb[i % 2][:, 0:128], start=True, stop=False))
            p.do("tensor", lambda e, s=s, h=h, ab=ab: e.matmul(
                av_banks[ab][:, 0:128], vbuf[:, s + 1, h * 128:(h + 1) * 128],
                pT_sb[i % 2][:, 128:256], start=False, stop=False))
            p.do("tensor", lambda e, s=s, h=h, ab=ab: e.matmul(
                av_banks[ab][:, 0:128], vper[:, h * 128:(h + 1) * 128],
                pT_sb[i % 2][0:16, 256:384], start=False, stop=True), inc="s_T")
            av_T[i] = p.c["s_T"]

        def emit_ao(i):
            s, h = it_sh(i)
            p.wait("scalar", "s_T", av_T[i])
            p.do("scalar", lambda e, s=s, h=h, i=i: e.copy(
                aoT[:, h, s * 128:(s + 1) * 128], av_banks[i % 2][:, 0:128]),
                inc="s_S")
            ao_V[i] = ("s_S", p.c["s_S"])

        def emit_inserts(l):
            for (b, kk) in ins_sched[l]:
                jj = w1q_slot[b]
                if kk == 0:
                    p.waitt("tensor", R_w1.loaded[jj])
                    if g1_prev[0] is not None:
                        p.wait("tensor", "s_S", g1_prev[0])
                p.do("tensor", lambda e, jj=jj, kk=kk: e.matmul(
                    ps_o2[1][:, 0:512], R_w1.bufs[jj][:, kk, :],
                    xnT[:, kk, :], start=(kk == 0), stop=(kk == 15)),
                    inc="s_T")
                if kk == 15:
                    R_w1.free[jj] = ("s_T", p.c["s_T"])
                    if b < NBLK - 3:
                        w1q_slot[b + 3] = ring_load(
                            R_w1, w1_t[b + 3], cast_at["w1"][0][b + 3],
                            queues=("scalar",))
                    dst = (kvlatT[:, b, :] if b < 4 else hT[:, b, :])
                    p.wait("scalar", "s_T", p.c["s_T"])
                    p.do("scalar", lambda e, dst=dst, b=b: e.activation(
                        out=dst, in_=ps_o2[1][:, 0:512],
                        func=AF.Gelu_apprx_tanh, bias=b1T[:, b:b + 1]),
                        inc="s_S")
                    g1_prev[0] = p.c["s_S"]

        for j in range(3):
            score_free[j] = ev_ring.free[j]
        for l in range(NIT + 5):
            emit_inserts(l)
            if l < NIT:
                emit_scores(l)
            if 5 <= l:
                emit_ao(l - 5)
            if 1 <= l <= NIT:
                emit_front_v(l - 1)
            if 2 <= l <= NIT + 1:
                emit_softmax_norm(l - 2)
            if 3 <= l <= NIT + 2:
                emit_back(l - 3)
            if 4 <= l <= NIT + 3:
                emit_av(l - 4)
            if 1 <= l <= NIT:
                emit_exp(l - 1)
        core_last_T = p.c["s_T"]
        core_last_S = p.c["s_S"]
        tr_free = [("s_S", pe_S[NIT - 2]), ("s_S", pe_S[NIT - 1])]
        ao_free = [ao_V[NIT - 2], ao_V[NIT - 1]]
        for j in range(3):
            last_i = max(i for i in range(NIT) if i % 3 == j)
            ev_ring.free[j] = ("s_S", ex_S[last_i])
        ev_ring.free[3] = ao_V[NIT - 1]

        for sl in range(8):
            jj = ring_load(R_wo, wo_t[sl], cast_at["wo"][sl], queues=qs2)
            p.waitt("tensor", R_wo.loaded[jj])
            for s in range(NSUB):
                bk = ev_next()
                if sl == 0 and s == 0:
                    p.wait("tensor", "s_S", core_last_S)
                    p.waitt("vector", acc_base_S)
                for c in range(16):
                    p.do("tensor", lambda e, jj=jj, c=c, s=s, bk=bk: e.matmul(
                        ev_ring.bufs[bk][:, 0:256],
                        aoT[:, c, s * 128:(s + 1) * 128],
                        R_wo.bufs[jj][:, c, :], start=(c == 0), stop=(c == 15)),
                        inc=("s_T" if c == 15 else None))
                g_T = p.c["s_T"]
                p.wait("vector", "s_T", g_T)
                p.chain("vector", "s_V",
                        lambda e, sl=sl, s=s, bk=bk: e.tensor_tensor(
                            acc[:, s, sl * 256:(sl + 1) * 256],
                            ev_ring.bufs[bk][:, 0:256],
                            acc[:, s, sl * 256:(sl + 1) * 256], ALU.add))
                ev_ring.free[bk] = ("s_V", p.c["s_V"])
            R_wo.free[jj] = ("s_T", p.c["s_T"])
        proj_last_T = p.c["s_T"]
        p.waitt("vector", acc_base_S)
        for s in range(NSUB):
            p.chain("vector", "s_V",
                    lambda e, s=s: e.scalar_tensor_tensor(
                        acc[:, s, :], b2bc[:], gcol[:, s:s + 1], acc[:, s, :],
                        ALU.mult, ALU.add))
        proj_sub_V = [p.c["s_V"]] * NSUB

        def emit_xside_prerun():
            nonlocal xin_side_next, ln_carry
            nb = (m + 1) // MPB
            nrow = ((m + 1) % MPB) * MEGA
            p.wait("scalar", "s_V", xside_consumed_V)
            p.wait("scalar", "s_S", core_last_S)
            p.wait("scalar", "s_T", core_last_T)
            if ins_on:
                p.wait("scalar", "s_T", pass2_T)
            p.dma("scalar", xside[:], x_d[nb, nrow:nrow + 128, :], "xs")
            xin_side_next = ("xs", p.c["xs"])

            p.waitt("vector", xin_side_next)
            p.wait("vector", "s_T", proj_last_T)
            p.waitt("vector", acc_base_S)
            p.chain("vector", "s_V",
                    lambda e: e.reduce_sum(sumx[:, 0:1], xside[:], axis=AX.X))
            pre_red = p.c["s_V"]
            p.waitt("scalar", xin_side_next)
            p.wait("scalar", "s_T", proj_last_T)
            p.waitt("scalar", lnT_done)
            p.do("scalar", lambda e: e.activation(
                out=aoT[:, 0:4, :], in_=xside[:], func=AF.Square,
                accum_out=sumsq[:, 0:1]), inc="s_S")
            pre_sq = p.c["s_S"]
            p.wait("vector", "s_S", pre_sq)
            p.chain("vector", "s_V",
                    lambda e: e.tensor_scalar(meanv[:, 0:1], sumx[:, 0:1],
                                              INV_D, None, ALU.mult))
            p.chain("vector", "s_V",
                    lambda e: e.tensor_scalar(varv[:, 0:1], sumsq[:, 0:1],
                                              INV_D, EPS, ALU.mult, ALU.add))
            p.chain("vector", "s_V",
                    lambda e: e.tensor_tensor(s1c[:, 0:1], meanv[:, 0:1],
                                              meanv[:, 0:1], ALU.mult))
            p.chain("vector", "s_V",
                    lambda e: e.tensor_tensor(varv[:, 0:1], varv[:, 0:1],
                                              s1c[:, 0:1], ALU.subtract))
            pre_mathA = p.c["s_V"]
            p.wait("scalar", "s_V", pre_mathA)
            p.do("scalar", lambda e: e.activation(
                out=stdv[:, 0:1], in_=varv[:, 0:1], func=AF.Sqrt), inc="s_S")
            pre_std = p.c["s_S"]
            p.wait("vector", "s_S", pre_std)
            p.chain("vector", "s_V",
                    lambda e: e.reciprocal(rstd[:, 0:1], stdv[:, 0:1]))
            p.chain("vector", "s_V",
                    lambda e: e.tensor_scalar(s0c[:, 0:1], rstd[:, 0:1], 1.0,
                                              None, ALU.add))
            p.chain("vector", "s_V",
                    lambda e: e.tensor_tensor(s1c[:, 0:1], meanv[:, 0:1],
                                              rstd[:, 0:1], ALU.mult))
            p.chain("vector", "s_V",
                    lambda e: e.tensor_scalar(s1c[:, 0:1], s1c[:, 0:1], -1.0,
                                              None, ALU.mult))
            pre_mathB = p.c["s_V"]
            p.chain("vector", "s_V",
                    lambda e: e.tensor_scalar(aoT[:, 0:4, :], xside[:],
                                              rstd[:, 0:1], s1c[:, 0:1],
                                              ALU.mult, ALU.add))
            ln_carry = (pre_red, pre_sq, pre_mathA, pre_std, pre_mathB,
                        ("s_V", p.c["s_V"]))

        acc_final_V = [0] * NSUB
        pass2_T = 0
        for q in range(4):
            jstart = NBLK if (q == 0 and ins_on) else 0
            for j in range(jstart, 16):
                jq = q * 16 + j
                jj = ring_load(R_w1, w1_t[jq], cast_at["w1"][q][j], queues=qs2)
                bk = ev_next()
                p.waitt("tensor", R_w1.loaded[jj])
                for c in range(16):
                    p.do("tensor", lambda e, jj=jj, c=c, bk=bk: e.matmul(
                        ev_ring.bufs[bk][:], R_w1.bufs[jj][:, c, :], xnT[:, c, :],
                        start=(c == 0), stop=(c == 15)),
                        inc=("s_T" if c == 15 else None))
                g_T = p.c["s_T"]
                R_w1.free[jj] = ("s_T", g_T)
                p.wait("scalar", "s_T", g_T)
                if j == jstart and q > 0:
                    p.wait("scalar", "s_T", pass2_T)
                if j == jstart and q == 0:
                    p.wait("scalar", "s_T", core_last_T)
                    p.waitt("scalar", lnT_done)
                p.do("scalar", lambda e, j=j, jq=jq, bk=bk: e.activation(
                    out=hT[:, j, :], in_=ev_ring.bufs[bk][:],
                    func=AF.Gelu_apprx_tanh, bias=b1T[:, jq:jq + 1]), inc="s_S")
                ev_ring.free[bk] = ("s_S", p.c["s_S"])
            hq_S = p.c["s_S"]
            for sl in range(8):
                jj = ring_load(R_w2, w2_t[q, sl], cast_at["w2"][q][sl],
                               queues=("sync", "scalar"))
                p.waitt("tensor", R_w2.loaded[jj])
                for s in range(NSUB):
                    bk = ev_next()
                    if sl == 0 and s == 0:
                        p.wait("tensor", "s_S", hq_S)
                    for c in range(16):
                        if q == 0 and ins_on and c < 4:
                            hop = kvlatT[:, c, s * 128:(s + 1) * 128]
                        else:
                            hop = hT[:, c, s * 128:(s + 1) * 128]
                        p.do("tensor",
                             lambda e, jj=jj, c=c, s=s, bk=bk, hop=hop:
                             e.matmul(
                                 ev_ring.bufs[bk][:, 0:256],
                                 hop, R_w2.bufs[jj][:, c, :],
                                 start=(c == 0), stop=(c == 15)),
                             inc=("s_T" if c == 15 else None))
                    g_T = p.c["s_T"]
                    p.wait("vector", "s_T", g_T)
                    if q == 0 and sl == 0:
                        p.wait("vector", "s_V", max(gcol_V, proj_sub_V[s]))
                    p.chain("vector", "s_V",
                            lambda e, sl=sl, s=s, bk=bk: e.scalar_tensor_tensor(
                                acc[:, s, sl * 256:(sl + 1) * 256],
                                ev_ring.bufs[bk][:, 0:256], gcol[:, s:s + 1],
                                acc[:, s, sl * 256:(sl + 1) * 256],
                                ALU.mult, ALU.add))
                    ev_ring.free[bk] = ("s_V", p.c["s_V"])
                    if q == 3:
                        acc_final_V[s] = p.c["s_V"]
                R_w2.free[jj] = ("s_T", p.c["s_T"])
            pass2_T = p.c["s_T"]
            if q == 0 and m + 1 < nmega:
                emit_xside_prerun()
        gcol_read_V = p.c["s_V"]
        hT_last_reader_T = p.c["s_T"]
        xnT_last_reader_T = p.c["s_T"]

        for s in range(NSUB):
            p.wait("scalar", "s_V", acc_final_V[s])
            sn = "o%d" % s
            p.dma("scalar", y_d[b, row0 + s * 128:row0 + (s + 1) * 128, :],
                  acc[:, s, :], sn)
            acc_out_at[s] = ("o%d" % s, p.c[sn])

        if m + 1 < nmega:
            nb = (m + 1) // MPB
            nrow = ((m + 1) % MPB) * MEGA
            for s in range(1, NSUB):
                p.waitt("scalar", acc_out_at[s])
                sn = "xi%d" % s
                p.dma("scalar", acc[:, s, :],
                      x_d[nb, nrow + s * 128:nrow + (s + 1) * 128, :], sn)
                xin_sub[s] = (sn, p.c[sn])
            xin_sub[0] = xin_side_next

    for s in range(NSUB):
        p.wait("sync", "o%d" % s, p.c["o%d" % s])

    with nc.Block() as block:
        @block.tensor
        def _(e):
            for f in p.ops["tensor"]:
                f(e)

        @block.vector
        def _(e):
            for f in p.ops["vector"]:
                f(e)

        @block.scalar
        def _(e):
            for f in p.ops["scalar"]:
                f(e)

        @block.sync
        def _(e):
            for f in p.ops["sync"]:
                f(e)

        @block.gpsimd
        def _(e):
            for f in p.ops["gpsimd"]:
                f(e)

    return nc


_NC_CACHE = {}


def kernel(L0_attn_x, L0_attn_Wq, L0_attn_Wkv, L0_attn_Wk, L0_attn_Wv, L0_attn_Wo,
           L0_ffn_router_weight, L0_ffn_router_bias, L0_ffn_w1, L0_ffn_b1,
           L0_ffn_w2, L0_ffn_b2):
    x = np.ascontiguousarray(np.asarray(L0_attn_x, dtype=np.float32))
    weights = {
        "wq": np.ascontiguousarray(np.asarray(L0_attn_Wq, dtype=np.float32)),
        "wkv": np.ascontiguousarray(np.asarray(L0_attn_Wkv, dtype=np.float32)),
        "wk": np.ascontiguousarray(np.asarray(L0_attn_Wk, dtype=np.float32)),
        "wv": np.ascontiguousarray(np.asarray(L0_attn_Wv, dtype=np.float32)),
        "wo": np.ascontiguousarray(np.asarray(L0_attn_Wo, dtype=np.float32)),
        "rw": np.ascontiguousarray(np.asarray(L0_ffn_router_weight, dtype=np.float32)),
        "rb": np.ascontiguousarray(np.asarray(L0_ffn_router_bias, dtype=np.float32)),
        "w1": np.ascontiguousarray(np.asarray(L0_ffn_w1, dtype=np.float32)),
        "b1": np.ascontiguousarray(np.asarray(L0_ffn_b1, dtype=np.float32)),
        "w2": np.ascontiguousarray(np.asarray(L0_ffn_w2, dtype=np.float32)),
        "b2": np.ascontiguousarray(np.asarray(L0_ffn_b2, dtype=np.float32)),
    }
    if "nc" not in _NC_CACHE:
        _NC_CACHE["nc"] = build()
    nc = _NC_CACHE["nc"]
    in_maps = []
    for c in range(NCORES):
        d = {"x": x[c * BPC:(c + 1) * BPC]}
        d.update(weights)
        in_maps.append(d)
    res = run_bass_kernel_spmd(nc, in_maps, core_ids=list(range(NCORES)))
    return np.concatenate([r["y"] for r in res.results], axis=0)


if __name__ == "__main__":
    rng = np.random.default_rng(0)
    ins = {
        "L0_attn_x": rng.standard_normal((16, S, Dm), dtype=np.float32),
        "L0_attn_Wq": rng.standard_normal((Dm, Dm), dtype=np.float32) / np.sqrt(Dm),
        "L0_attn_Wkv": rng.standard_normal((Dm, L), dtype=np.float32) / np.sqrt(Dm),
        "L0_attn_Wk": rng.standard_normal((L, Dm), dtype=np.float32) / np.sqrt(L),
        "L0_attn_Wv": rng.standard_normal((L, Dm), dtype=np.float32) / np.sqrt(L),
        "L0_attn_Wo": rng.standard_normal((Dm, Dm), dtype=np.float32) / np.sqrt(Dm),
        "L0_ffn_router_weight": rng.standard_normal(Dm).astype(np.float32) * 0.02,
        "L0_ffn_router_bias": rng.standard_normal(1).astype(np.float32) * 0.1,
        "L0_ffn_w1": rng.standard_normal((Dm, DFF), dtype=np.float32) / np.sqrt(Dm),
        "L0_ffn_b1": rng.standard_normal(DFF).astype(np.float32) * 0.1,
        "L0_ffn_w2": rng.standard_normal((DFF, Dm), dtype=np.float32) / np.sqrt(DFF),
        "L0_ffn_b2": rng.standard_normal(Dm).astype(np.float32) * 0.1,
    }
    out = kernel(**ins)
    print(out.shape, out.dtype, np.abs(out).mean())


# version 37 (final)
# speedup vs baseline: 1.3919x; 1.3919x over previous
import numpy as np
import ml_dtypes
import concourse.bass as bass
import concourse.mybir as mybir
from concourse.bass_utils import run_bass_kernel_spmd

F32 = mybir.dt.float32
BF16 = mybir.dt.bfloat16
AF = mybir.ActivationFunctionType
ALU = mybir.AluOpType
AX = mybir.AxisListType
BF = ml_dtypes.bfloat16

NCORES = 8
BPC = 2
S = 2048
Dm = 2048
DFF = 8192
L = 512
H = 16
HD = 128
WIN = 128
PER = 16
MEGA = 512
NSUB = 4
NMEGA = BPC * S // MEGA
MPB = S // MEGA
DQ = DFF // 4
SCALE = float(1.0 / np.sqrt(HD))
NEGINF = -1.0e30
EPS = 1e-5
INV_D = float(1.0 / Dm)


def _build_masks():
    m = np.full((3, 128, 272), NEGINF, dtype=np.float32)
    r = np.arange(128)[:, None]
    j = np.arange(256)[None, :]
    allow0 = (j >= 128) & (j <= 128 + r)
    m[0, :, :256][allow0] = 0.0
    allow1 = (j <= 15) | ((j >= r + 1) & (j <= r + 128))
    m[1, :, :256][allow1] = 0.0
    allow2 = (j >= r + 1) & (j <= r + 128)
    m[2, :, :256][allow2] = 0.0
    m[2, :, 256:] = 0.0
    return m.astype(BF)


class Prog:
    def __init__(self, nc):
        self.nc = nc
        self.ops = {k: [] for k in ("tensor", "vector", "scalar", "sync", "gpsimd")}
        self.sems = {}
        self.c = {}

    def sem(self, name):
        self.sems[name] = self.nc.alloc_semaphore(name)
        self.c[name] = 0

    def wait(self, eng, name, val):
        if val <= 0:
            return
        assert val <= self.c[name], (eng, name, val, self.c[name])
        h = self.sems[name]
        self.ops[eng].append(lambda e, h=h, v=val: e.wait_ge(h, v))

    def waitt(self, eng, ms):
        if ms is not None:
            self.wait(eng, ms[0], ms[1])

    def chain(self, eng, sem, fn):
        self.wait(eng, sem, self.c[sem])
        return self.do(eng, fn, inc=sem)

    def do(self, eng, fn, inc=None, n=1):
        if inc is None:
            self.ops[eng].append(fn)
            return None
        h = self.sems[inc]
        self.c[inc] += n
        self.ops[eng].append(lambda e, fn=fn, h=h, n=n: fn(e).then_inc(h, n))
        return self.c[inc]

    def dma(self, eng, out, in_, sname):
        return self.do(eng, lambda e, o=out, i=in_: e.dma_start(out=o, in_=i),
                       inc=sname, n=16)


class Ring:
    def __init__(self, p, name, bufs):
        self.p = p
        self.name = name
        self.bufs = bufs
        self.n = len(bufs)
        self.i = 0
        self.free = [None] * self.n
        self.sems = []
        for j in range(self.n):
            sn = "%s%d" % (name, j)
            p.sem(sn)
            self.sems.append(sn)
        self.loaded = [(self.sems[j], 0) for j in range(self.n)]

    def next(self):
        j = self.i % self.n
        self.i += 1
        return j


def build(nmega=NMEGA, phases="LAF"):
    nc = bass.Bass("TRN2", target_bir_lowering=False)
    p = Prog(nc)
    for s in ("s_T", "s_V", "s_S",
              "c_const", "c_sm", "c_b1",
              "lnT",
              "xs", "xi1", "xi2", "xi3",
              "o0", "o1", "o2", "o3"):
        p.sem(s)

    x_d = nc.dram_tensor("x", [BPC, S, Dm], F32, kind="ExternalInput").ap()
    wq_d = nc.dram_tensor("wq", [Dm, Dm], F32, kind="ExternalInput").ap()
    wkv_d = nc.dram_tensor("wkv", [Dm, L], F32, kind="ExternalInput").ap()
    wk_d = nc.dram_tensor("wk", [L, Dm], F32, kind="ExternalInput").ap()
    wv_d = nc.dram_tensor("wv", [L, Dm], F32, kind="ExternalInput").ap()
    wo_d = nc.dram_tensor("wo", [Dm, Dm], F32, kind="ExternalInput").ap()
    rw_d = nc.dram_tensor("rw", [Dm], F32, kind="ExternalInput").ap()
    rb_d = nc.dram_tensor("rb", [1], F32, kind="ExternalInput").ap()
    w1_d = nc.dram_tensor("w1", [Dm, DFF], F32, kind="ExternalInput").ap()
    b1_d = nc.dram_tensor("b1", [DFF], F32, kind="ExternalInput").ap()
    w2_d = nc.dram_tensor("w2", [DFF, Dm], F32, kind="ExternalInput").ap()
    b2_d = nc.dram_tensor("b2", [Dm], F32, kind="ExternalInput").ap()
    y_d = nc.dram_tensor("y", [BPC, S, Dm], F32, kind="ExternalOutput").ap()

    wkv_t = nc.dram_tensor("wkv_t", [4, 128, 16, 128], BF16).ap()
    wq_t = nc.dram_tensor("wq_t", [16, 128, 16, 128], BF16).ap()
    wk_t = nc.dram_tensor("wk_t", [16, 128, 4, 128], BF16).ap()
    wv_t = nc.dram_tensor("wv_t", [8, 128, 4, 256], BF16).ap()
    wo_t = nc.dram_tensor("wo_t", [8, 128, 16, 256], BF16).ap()
    w1_t = nc.dram_tensor("w1_t", [64, 128, 16, 128], BF16).ap()
    w2_t = nc.dram_tensor("w2_t", [4, 8, 128, 16, 256], BF16).ap()

    id_bf_d = nc.inline_tensor(np.eye(128, dtype=BF), "id_bf").ap()
    id_f32_d = nc.inline_tensor(np.eye(128, dtype=np.float32), "id_f32").ap()
    masks_d = nc.inline_tensor(_build_masks(), "masks").ap()

    _BASE = 16512

    def A(name, shape, dtype, offset):
        return nc.alloc_sbuf_tensor_at(name, shape, dtype, offset=_BASE + offset)
    acc = A("acc", [128, NSUB, Dm], F32, offset=0)
    xnT = A("xnT", [128, 16, MEGA], BF16, offset=32768)
    kbuf = A("kbuf", [128, 16, 640], BF16, offset=49152)
    vbuf = A("vbuf", [128, 5, Dm], BF16, offset=69632)
    qT = A("qT", [128, 16, MEGA], BF16, offset=90112)
    hT = A("hT", [128, 16, MEGA], BF16, offset=90112)
    aoT = A("aoT", [128, 16, MEGA], BF16, offset=106496)
    xside = A("xside", [128, Dm], F32, offset=122880)
    p_sb = [A("p_sb%d" % j, [128, 272], BF16, offset=122880 + 576 * j)
            for j in range(4)]
    pT_sb = [A("pT_sb0", [128, 384], BF16, offset=125184),
             A("pT_sb1", [128, 384], BF16, offset=125952)]
    kvlatT = A("kvlatT", [128, 4, MEGA], BF16, offset=126720)
    wkv_s = [A("wkv_s0", [128, 16, 128], BF16, offset=131072),
             A("wkv_s1", [128, 16, 128], BF16, offset=135168)]
    wq_s = [A("wq_s0", [128, 16, 128], BF16, offset=139264),
            A("wq_s1", [128, 16, 128], BF16, offset=143360)]
    wk_s = [A("wk_s0", [128, 4, 128], BF16, offset=147456),
            A("wk_s1", [128, 4, 128], BF16, offset=148480)]
    wv_s = [A("wv_s0", [128, 4, 256], BF16, offset=149504),
            A("wv_s1", [128, 4, 256], BF16, offset=151552)]
    wo_s = [A("wo_s0", [128, 16, 256], BF16, offset=153600),
            A("wo_s1", [128, 16, 256], BF16, offset=161792)]
    w1_s = [A("w1_s0", [128, 16, 128], BF16, offset=169984),
            A("w1_s1", [128, 16, 128], BF16, offset=174080),
            A("w1_s2", [128, 16, 128], BF16, offset=194560)]
    w2_s = [A("w2_s0", [128, 16, 256], BF16, offset=178176),
            A("w2_s1", [128, 16, 256], BF16, offset=186368)]
    stg_bf = A("stg_bf", [128, 128], BF16, offset=194560)
    stg_f32 = A("stg_f32", [128, 128], F32, offset=194560)
    vper = A("vper", [16, Dm], BF16, offset=198656)
    kper = A("kper", [128, 16, 16], BF16, offset=202752)
    masks_sb = A("masks_sb", [128, 3, 272], BF16, offset=203264)
    id_bf = A("id_bf_sb", [128, 128], BF16, offset=204928)
    id_f32 = A("id_f32_sb", [128, 128], F32, offset=205184)
    ones_row = A("ones_row", [1, 128], BF16, offset=205696)
    b1T = A("b1T", [128, 64], F32, offset=205952)
    rwT = A("rwT", [128, 16], BF16, offset=206208)
    b2bc = A("b2bc", [128, Dm], BF16, offset=206272)
    rb_sb = A("rb_sb", [1, 1], BF16, offset=210368)
    rbh = A("rbh", [128, 1], F32, offset=210432)
    sumx = A("sumx", [128, NSUB], F32, offset=210496)
    sumsq = A("sumsq", [128, NSUB], F32, offset=210560)
    meanv = A("meanv", [128, NSUB], F32, offset=210624)
    varv = A("varv", [128, NSUB], F32, offset=210688)
    stdv = A("stdv", [128, NSUB], F32, offset=210752)
    rstd = A("rstd", [128, NSUB], F32, offset=210816)
    s0c = A("s0c", [128, NSUB], F32, offset=210880)
    s1c = A("s1c", [128, NSUB], F32, offset=210944)
    tstat = A("tstat", [128, NSUB], F32, offset=211008)
    gcol = A("gcol", [128, NSUB], F32, offset=211072)
    negmax = [A("negmax%d" % j, [128, 1], F32, offset=211136 + 64 * j)
              for j in range(4)]
    rsum = [A("rsum%d" % j, [128, 1], F32, offset=211392 + 64 * j)
            for j in range(4)]
    rinv = [A("rinv%d" % j, [128, 1], F32, offset=211648 + 64 * j)
            for j in range(4)]

    ps_ev = nc.alloc_psum_tensor("ps_ev", [128, 512], F32)
    ps_ev2 = nc.alloc_psum_tensor("ps_ev2", [128, 512], F32)
    ps_s = [nc.alloc_psum_tensor("ps_sA", [128, 512], F32),
            nc.alloc_psum_tensor("ps_sB", [128, 512], F32)]
    ps_tr2 = [nc.alloc_psum_tensor("ps_trA", [128, 1024], BF16),
              nc.alloc_psum_tensor("ps_trB", [128, 1024], BF16)]
    ps_o2 = [nc.alloc_psum_tensor("ps_oA", [128, 512], F32),
             nc.alloc_psum_tensor("ps_oB", [128, 512], F32)]

    ev_ring = Ring(p, "ev", [ps_ev, ps_ev2, ps_s[0], ps_s[1]])
    ps_s4 = ev_ring.bufs
    score_free = [None, None, None, None]

    p.dma("sync", id_bf[:], id_bf_d, "c_const")
    p.dma("sync", id_f32[:], id_f32_d, "c_const")
    for i in range(3):
        p.dma("sync", masks_sb[:, i, :], masks_d[i], "c_const")
    c_const = p.c["c_const"]
    p.do("vector", lambda e: e.memset(ones_row[:], 1.0), inc="s_V")
    p.do("vector", lambda e: e.memset(stg_bf[:], 0.0), inc="s_V")
    v_stg0 = p.c["s_V"]
    p.wait("tensor", "s_V", v_stg0)
    p.wait("tensor", "c_const", c_const)
    p.do("tensor", lambda e: e.transpose(ps_tr2[0][:, 256:384], stg_bf[:],
                                         id_bf[:]))
    p.do("tensor", lambda e: e.transpose(ps_tr2[1][:, 256:384], stg_bf[:],
                                         id_bf[:]), inc="s_T")
    t_trz = p.c["s_T"]
    p.wait("gpsimd", "s_V", v_stg0)
    p.wait("gpsimd", "s_T", t_trz)
    p.dma("gpsimd", stg_bf[0:16, :], rw_d.rearrange("(p c) -> p c", c=128), "c_sm")
    p.dma("gpsimd", rb_sb[:], rb_d[None, :], "c_sm")
    b2stage = A("b2stage", [1, Dm], BF16, offset=90112)
    p.dma("gpsimd", b2stage[:], b2_d[None, :], "c_sm")
    c_smalls = p.c["c_sm"]
    cast_at = {}

    def cast_class(base, dst, src, n, grp=4):
        lst = []
        for g0 in range(0, n, grp):
            sn = "%s_g%d" % (base, g0 // grp)
            p.sem(sn)
            for j in range(g0, min(g0 + grp, n)):
                p.dma("gpsimd", dst[j], src[j], sn)
            tot = p.c[sn]
            for j in range(g0, min(g0 + grp, n)):
                lst.append((sn, tot))
        return lst

    wkvr = wkv_d.rearrange("(c p) (j m) -> j p c m", p=128, m=128)
    cast_at["wkv"] = cast_class("c_wkv", wkv_t, wkvr, 4)
    wqr = wq_d.rearrange("(c p) (j m) -> j p c m", p=128, m=128)
    cast_at["wq"] = cast_class("c_wq", wq_t, wqr, 16)
    wkr = wk_d.rearrange("(c p) (j m) -> j p c m", p=128, m=128)
    cast_at["wk"] = cast_class("c_wk", wk_t, wkr, 16)
    wvr = wv_d.rearrange("(c p) (s n) -> s p c n", p=128, n=256)
    cast_at["wv"] = cast_class("c_wv", wv_t, wvr, 8)
    w1r = w1_d.rearrange("(c p) (j m) -> j p c m", p=128, m=128)
    w2r = w2_d.rearrange("(q c p) (s n) -> q s p c n", q=4, p=128, n=256)
    wor = wo_d.rearrange("(c p) (s n) -> s p c n", p=128, n=256)

    def cast_w1(jq):
        return cast_class("c_w1_%d" % jq, w1_t[jq * 16:(jq + 1) * 16],
                          w1r[jq * 16:(jq + 1) * 16], 16)

    def cast_w2(q):
        return cast_class("c_w2_%d" % q, w2_t[q], w2r[q], 8)

    cast_at["w1"] = [None] * 4
    cast_at["w2"] = [None] * 4
    cast_at["w1"][0] = cast_w1(0)
    cast_at["wo"] = cast_class("c_wo", wo_t, wor, 8)
    cast_at["w2"][0] = cast_w2(0)
    for q in range(1, 4):
        cast_at["w1"][q] = cast_w1(q)
        cast_at["w2"][q] = cast_w2(q)

    p.wait("tensor", "c_sm", c_smalls)
    p.wait("tensor", "c_const", c_const)
    p.wait("tensor", "s_V", v_stg0)
    p.do("tensor", lambda e: e.transpose(ps_tr2[0][:, 0:128], stg_bf[:], id_bf[:]),
         inc="s_T")
    t_rw = p.c["s_T"]
    p.wait("vector", "s_T", t_rw)
    p.do("vector", lambda e: e.tensor_copy(rwT[:], ps_tr2[0][:, 0:16]), inc="s_V")
    v_rwT = p.c["s_V"]
    p.wait("vector", "s_T", t_rw)
    p.do("vector", lambda e: e.memset(stg_f32[:], 0.0), inc="s_V")
    v_stgf = p.c["s_V"]
    p.wait("sync", "s_V", v_stgf)
    p.dma("sync", stg_f32[0:64, :], b1_d.rearrange("(p c) -> p c", c=128), "c_b1")
    c_b1 = p.c["c_b1"]
    p.wait("tensor", "c_b1", c_b1)
    p.wait("tensor", "s_V", v_stgf)
    p.do("tensor", lambda e: e.transpose(ps_o2[0][:, 0:128], stg_f32[:], id_f32[:]),
         inc="s_T")
    t_b1 = p.c["s_T"]
    p.wait("vector", "s_T", t_b1)
    p.do("vector", lambda e: e.tensor_copy(b1T[:], ps_o2[0][:, 0:64]), inc="s_V")
    v_b1T = p.c["s_V"]
    p.wait("tensor", "s_V", v_stg0)
    p.wait("tensor", "s_V", v_b1T)
    p.do("tensor", lambda e: e.matmul(ps_o2[0][:, 128:129], ones_row[:], rb_sb[:],
                                      start=True, stop=True), inc="s_T")
    t_rb = p.c["s_T"]
    p.wait("vector", "s_T", t_rb)
    p.do("vector", lambda e: e.tensor_scalar(rbh[:], ps_o2[0][:, 128:129], 0.5,
                                             None, ALU.mult), inc="s_V")
    v_rbh = p.c["s_V"]
    v_prev = v_rbh
    for j in range(4):
        p.wait("tensor", "s_V", v_prev)
        p.do("tensor", lambda e, j=j: e.matmul(
            ps_o2[j % 2][:, 0:512], ones_row[:],
            b2stage[0:1, j * 512:(j + 1) * 512], start=True, stop=True),
            inc="s_T")
        t_j = p.c["s_T"]
        p.wait("vector", "s_T", t_j)
        p.chain("vector", "s_V", lambda e, j=j: e.tensor_copy(
            b2bc[:, j * 512:(j + 1) * 512], ps_o2[j % 2][:, 0:512]))
        v_prev = p.c["s_V"]
    t_b2bc = p.c["s_T"]
    v_b2bc = p.c["s_V"]
    ao_free = [("s_V", v_b2bc), ("s_V", v_b2bc)]
    tr_free = [("s_V", v_rwT), None]


    p.dma("scalar", xside[:], x_d[0, 0:128, :], "xs")
    xin_sub = [None] * NSUB
    xin_sub[0] = ("xs", p.c["xs"])
    for s in range(1, NSUB):
        sn = "xi%d" % s
        p.dma("scalar", acc[:, s, :], x_d[0, s * 128:(s + 1) * 128, :], sn)
        xin_sub[s] = (sn, p.c[sn])

    R_wkv = Ring(p, "wkv", wkv_s)
    R_wq = Ring(p, "wq", wq_s)
    R_wk = Ring(p, "wk", wk_s)
    R_wv = Ring(p, "wv", wv_s)
    R_wo = Ring(p, "wo", wo_s)
    R_w1 = Ring(p, "w1", w1_s)
    R_w1.free[2] = ("s_T", t_b1)
    R_w2 = Ring(p, "w2", w2_s)

    def ring_load(ring, src_ap, cast_need, queues=("sync",)):
        j = ring.next()
        qe = queues[(ring.i - 1) % len(queues)]
        if ring.free[j] is not None:
            p.wait(qe, ring.free[j][0], ring.free[j][1])
        p.waitt(qe, cast_need)
        cnt = p.dma(qe, ring.bufs[j][:], src_ap, ring.sems[j])
        ring.loaded[j] = (ring.sems[j], cnt)
        return j

    def ev_next():
        j = ev_ring.next()
        if ev_ring.free[j] is not None:
            p.wait("tensor", ev_ring.free[j][0], ev_ring.free[j][1])
        return j

    xnT_last_reader_T = 0
    hT_last_reader_T = t_b2bc
    core_last_T = 0
    acc_out_at = [None] * NSUB
    xside_consumed_V = 0
    aoT_last_reader_T = 0
    gcol_read_V = 0
    ln_V_done = 0
    acc_base_S = None
    ln_carry = None
    xin_side_next = None
    g1_prev = [None, None]
    proj_last_T = 0
    kv_written_V = 0

    for m in range(nmega):
        b = m // MPB
        mm = m % MPB
        row0 = mm * MEGA

        xnb = [aoT[:, 0:4, :]] + [hT[:, 4 * s:4 * (s + 1), :]
                                  for s in range(1, NSUB)]
        red_V = [0] * NSUB
        sq_S = [0] * NSUB
        mathA_V = [0] * NSUB
        std_S = [0] * NSUB
        mathB_V = [0] * NSUB
        xn_S = [0] * NSUB
        fold_S = [0] * NSUB
        hb = [0]

        def heartbeat():
            p.wait("tensor", "s_V", p.c["s_V"])
            p.do("tensor", lambda e: e.matmul(
                ps_o2[1][:, 264:265], ones_row[:], ones_row[0:1, 0:1],
                start=True, stop=True))

        def ln_stage_a(u):
            xa = (xside[:] if u == 0 else acc[:, u, :])
            p.waitt("vector", xin_sub[u])
            p.chain("vector", "s_V",
                    lambda e, xa=xa, u=u: e.reduce_sum(sumx[:, u:u + 1], xa,
                                                       axis=AX.X))
            red_V[u] = p.c["s_V"]
            p.waitt("scalar", xin_sub[u])
            if u == 0:
                p.wait("scalar", "s_T", proj_last_T)
                p.wait("scalar", "s_V", ln_V_done)
            if u == 1:
                p.wait("scalar", "s_T", hT_last_reader_T)
            p.do("scalar", lambda e, xa=xa, u=u: e.activation(
                out=xnb[u][:], in_=xa, func=AF.Square,
                accum_out=sumsq[:, u:u + 1]), inc="s_S")
            sq_S[u] = p.c["s_S"]

        def ln_stage_b(u):
            p.wait("vector", "s_S", sq_S[u])
            p.chain("vector", "s_V",
                    lambda e, u=u: e.tensor_scalar(
                        meanv[:, u:u + 1], sumx[:, u:u + 1], INV_D, None,
                        ALU.mult))
            p.chain("vector", "s_V",
                    lambda e, u=u: e.tensor_scalar(
                        varv[:, u:u + 1], sumsq[:, u:u + 1], INV_D, EPS,
                        ALU.mult, ALU.add))
            p.chain("vector", "s_V",
                    lambda e, u=u: e.tensor_tensor(
                        s1c[:, u:u + 1], meanv[:, u:u + 1], meanv[:, u:u + 1],
                        ALU.mult))
            p.chain("vector", "s_V",
                    lambda e, u=u: e.tensor_tensor(
                        varv[:, u:u + 1], varv[:, u:u + 1], s1c[:, u:u + 1],
                        ALU.subtract))
            mathA_V[u] = p.c["s_V"]
            p.wait("scalar", "s_V", mathA_V[u])
            p.do("scalar", lambda e, u=u: e.activation(
                out=stdv[:, u:u + 1], in_=varv[:, u:u + 1], func=AF.Sqrt),
                inc="s_S")
            std_S[u] = p.c["s_S"]

        def ln_stage_c(u, skip_xn=False):
            xa = (xside[:] if u == 0 else acc[:, u, :])
            if not skip_xn:
                p.wait("vector", "s_S", std_S[u])
                p.chain("vector", "s_V",
                        lambda e, u=u: e.reciprocal(rstd[:, u:u + 1],
                                                    stdv[:, u:u + 1]))
                p.chain("vector", "s_V",
                        lambda e, u=u: e.tensor_scalar(
                            s0c[:, u:u + 1], rstd[:, u:u + 1], 1.0, None,
                            ALU.add))
                p.chain("vector", "s_V",
                        lambda e, u=u: e.tensor_tensor(
                            s1c[:, u:u + 1], meanv[:, u:u + 1],
                            rstd[:, u:u + 1], ALU.mult))
                p.chain("vector", "s_V",
                        lambda e, u=u: e.tensor_scalar(
                            s1c[:, u:u + 1], s1c[:, u:u + 1], -1.0, None,
                            ALU.mult))
                mathB_V[u] = p.c["s_V"]
                p.chain("vector", "s_V",
                        lambda e, xa=xa, u=u: e.tensor_scalar(
                            xnb[u][:], xa, rstd[:, u:u + 1], s1c[:, u:u + 1],
                            ALU.mult, ALU.add))
                xn_S[u] = ("s_V", p.c["s_V"])
            p.waitt("scalar", xn_S[u])
            if u == 0:
                p.waitt("scalar", acc_out_at[0])
            p.do("scalar", lambda e, xa=xa, u=u: e.activation(
                out=acc[:, u, :], in_=xa, func=AF.Identity,
                scale=s0c[:, u:u + 1], bias=s1c[:, u:u + 1]), inc="s_S")
            fold_S[u] = p.c["s_S"]

        def ln_stage_d(u):
            p.waitt("tensor", xn_S[u])
            for c in range(16):
                tb = c % 2
                p.waitt("tensor", tr_free[tb])
                p.do("tensor", lambda e, u=u, c=c, tb=tb: e.transpose(
                    ps_tr2[tb][:, 0:128],
                    xnb[u][:, c // 4, (c % 4) * 128:(c % 4 + 1) * 128],
                    id_bf[:]), inc="s_T")
                tr_T = p.c["s_T"]
                p.wait("vector", "s_T", tr_T)
                if u == 0 and c == 0:
                    p.wait("vector", "s_T", xnT_last_reader_T)
                p.do("vector", lambda e, u=u, c=c, tb=tb: e.tensor_copy(
                    xnT[:, c, u * 128:(u + 1) * 128],
                    ps_tr2[tb][:, 0:128]), inc="s_V")
                tr_free[tb] = ("s_V", p.c["s_V"])
                if c % 8 == 7:
                    heartbeat()

        if ln_carry is not None:
            (red_V[0], sq_S[0], mathA_V[0], std_S[0], mathB_V[0],
             xn_S[0]) = ln_carry
        for u in range(NSUB + 2):
            if u < NSUB and not (u == 0 and ln_carry is not None):
                ln_stage_a(u)
            if 1 <= u <= NSUB and not (u == 1 and ln_carry is not None):
                ln_stage_b(u - 1)
            if u >= 2:
                ln_stage_c(u - 2, skip_xn=(u == 2 and ln_carry is not None))
                ln_stage_d(u - 2)
        ln_carry = None
        xside_consumed_V = red_V[0]
        acc_base_S = ("s_S", fold_S[NSUB - 1])
        ln_V_done = p.c["s_V"]
        lnT_done = ("s_V", p.c["s_V"])
        lnT_prev = lnT_done

        p.wait("vector", "s_T", core_last_T)
        p.wait("vector", "s_V", kv_written_V)
        if mm == 0:
            p.do("vector", lambda e: e.memset(kbuf[:, :, 0:128], 0.0), inc="s_V")
            p.do("vector", lambda e: e.memset(vbuf[:, 0, :], 0.0), inc="s_V")
        else:
            p.do("vector", lambda e: e.tensor_copy(kbuf[:, :, 0:128],
                                                   kbuf[:, :, 512:640]), inc="s_V")
            p.do("vector", lambda e: e.tensor_copy(vbuf[:, 0, :], vbuf[:, 4, :]),
                 inc="s_V")
        roll_V = p.c["s_V"]

        p.waitt("tensor", ao_free[0])
        p.waitt("tensor", lnT_done)
        p.wait("tensor", "s_V", v_rwT)
        for s in range(NSUB):
            for c in range(16):
                p.do("tensor", lambda e, s=s, c=c: e.matmul(
                    ps_o2[0][:, s:s + 1], xnT[:, c, s * 128:(s + 1) * 128],
                    rwT[:, c:c + 1], start=(c == 0), stop=(c == 15)),
                    inc=("s_T" if c == 15 else None))
        gate_T = p.c["s_T"]
        p.wait("scalar", "s_T", gate_T)
        p.wait("scalar", "s_V", max(v_rbh, gcol_read_V))
        p.do("scalar", lambda e: e.activation(out=tstat[:], in_=ps_o2[0][:, 0:NSUB],
                                              func=AF.Tanh, bias=rbh[:], scale=0.5),
             inc="s_S")
        gate_S = p.c["s_S"]
        p.wait("vector", "s_S", gate_S)
        p.wait("vector", "s_V", gcol_read_V)
        p.do("vector", lambda e: e.tensor_scalar(gcol[:], tstat[:], 0.5, 0.5,
                                                 ALU.mult, ALU.add), inc="s_V")
        gcol_V = p.c["s_V"]
        ao_free[0] = ("s_S", gate_S)

        qs2 = ("sync",)
        ins_on = m > 0

        kvlat_V = []
        for j in range(4):
            jj = ring_load(R_wkv, wkv_t[j], cast_at["wkv"][j], queues=qs2)
            bk = ev_next()
            p.waitt("tensor", R_wkv.loaded[jj])
            p.waitt("tensor", lnT_done)
            for c in range(16):
                p.do("tensor", lambda e, jj=jj, c=c, bk=bk: e.matmul(
                    ev_ring.bufs[bk][:], R_wkv.bufs[jj][:, c, :], xnT[:, c, :],
                    start=(c == 0), stop=(c == 15)),
                    inc=("s_T" if c == 15 else None))
            g_T = p.c["s_T"]
            R_wkv.free[jj] = ("s_T", g_T)
            p.wait("vector", "s_T", g_T)
            p.do("vector", lambda e, j=j, bk=bk: e.tensor_copy(
                kvlatT[:, j, :], ev_ring.bufs[bk][:]), inc="s_V")
            kvlat_V.append(p.c["s_V"])
            ev_ring.free[bk] = ("s_V", p.c["s_V"])
        kvlat_ready_V = p.c["s_V"]

        for j in range(16):
            jj = ring_load(R_wq, wq_t[j], cast_at["wq"][j], queues=qs2)
            bk = ev_next()
            p.waitt("tensor", R_wq.loaded[jj])
            if j == 0:
                p.waitt("tensor", lnT_done)
            for c in range(16):
                p.do("tensor", lambda e, jj=jj, c=c, bk=bk: e.matmul(
                    ev_ring.bufs[bk][:], R_wq.bufs[jj][:, c, :], xnT[:, c, :],
                    start=(c == 0), stop=(c == 15)),
                    inc=("s_T" if c == 15 else None))
            g_T = p.c["s_T"]
            R_wq.free[jj] = ("s_T", g_T)
            p.wait("vector", "s_T", g_T)
            if j == 0:
                p.wait("vector", "s_T", hT_last_reader_T)
            p.do("vector", lambda e, j=j, bk=bk: e.tensor_scalar(
                qT[:, j, :], ev_ring.bufs[bk][:], SCALE, None, ALU.mult),
                inc="s_V")
            ev_ring.free[bk] = ("s_V", p.c["s_V"])
        qT_ready_V = p.c["s_V"]

        for j in range(16):
            jj = ring_load(R_wk, wk_t[j], cast_at["wk"][j], queues=qs2)
            bk = ev_next()
            p.waitt("tensor", R_wk.loaded[jj])
            if j == 0:
                p.wait("tensor", "s_V", kvlat_ready_V)
            for c in range(4):
                p.do("tensor", lambda e, jj=jj, c=c, bk=bk: e.matmul(
                    ev_ring.bufs[bk][:], R_wk.bufs[jj][:, c, :], kvlatT[:, c, :],
                    start=(c == 0), stop=(c == 3)),
                    inc=("s_T" if c == 3 else None))
            g_T = p.c["s_T"]
            R_wk.free[jj] = ("s_T", g_T)
            p.wait("vector", "s_T", g_T)
            if j == 0:
                p.wait("vector", "s_V", roll_V)
            p.do("vector", lambda e, j=j, bk=bk: e.tensor_copy(
                kbuf[:, j, 128:640], ev_ring.bufs[bk][:]), inc="s_V")
            ev_ring.free[bk] = ("s_V", p.c["s_V"])
        k_ready_V = p.c["s_V"]

        for j in range(8):
            jj = ring_load(R_wv, wv_t[j], cast_at["wv"][j], queues=qs2)
            p.waitt("tensor", R_wv.loaded[jj])
            for s in range(NSUB):
                bk = ev_next()
                for c in range(4):
                    p.do("tensor", lambda e, jj=jj, c=c, s=s, bk=bk: e.matmul(
                        ev_ring.bufs[bk][:, 0:256],
                        kvlatT[:, c, s * 128:(s + 1) * 128],
                        R_wv.bufs[jj][:, c, :],
                        start=(c == 0), stop=(c == 3)),
                        inc=("s_T" if c == 3 else None))
                g_T = p.c["s_T"]
                p.wait("vector", "s_T", g_T)
                if j == 0 and s == 0:
                    p.wait("vector", "s_V", roll_V)
                p.do("vector", lambda e, j=j, s=s, bk=bk: e.tensor_copy(
                    vbuf[:, s + 1, j * 256:(j + 1) * 256],
                    ev_ring.bufs[bk][:, 0:256]), inc="s_V")
                ev_ring.free[bk] = ("s_V", p.c["s_V"])
            R_wv.free[jj] = ("s_T", p.c["s_T"])
        v_ready_V = p.c["s_V"]

        w1q_slot = {}
        if ins_on:
            for b3 in range(3):
                w1q_slot[b3] = ring_load(R_w1, w1_t[b3], cast_at["w1"][0][b3],
                                         queues=("scalar",))

        if mm == 0:
            p.chain("vector", "s_V",
                    lambda e: e.tensor_copy(kper[:], kbuf[:, :, 128:144]))
            p.chain("vector", "s_V",
                    lambda e: e.tensor_copy(vper[:], vbuf[0:16, 1, :]))
        per_V = p.c["s_V"]
        kv_written_V = p.c["s_V"]

        NIT = NSUB * H
        NBLK = 8
        ins_sched = [[] for _ in range(NIT + 5)]
        if ins_on:
            pairs = [(b, kk) for b in range(NBLK) for kk in range(16)]
            li = 0
            for l in range(64):
                for _ in range(2):
                    ins_sched[l].append(pairs[li])
                    li += 1
            assert li == NBLK * 16

        st_T = [0] * NIT
        ma_V = [0] * NIT
        ex_S = [0] * NIT
        ts_V = [0] * NIT
        tr_T = [0] * NIT
        pe_S = [0] * NIT
        av_T = [0] * NIT
        ao_V = [None] * NIT

        def it_sh(i):
            return i % NSUB, i // NSUB

        def emit_scores(i):
            s, h = it_sh(i)
            b3 = i % 3
            if i >= 3:
                p.wait("tensor", "s_S", ex_S[i - 3])
            else:
                p.waitt("tensor", score_free[b3])
                if i == 0:
                    p.wait("tensor", "s_V", max(k_ready_V, v_ready_V, per_V,
                                                qT_ready_V))
            p.do("tensor", lambda e, s=s, h=h, b3=b3: e.matmul(
                ps_s4[b3][:, 0:256], qT[:, h, s * 128:(s + 1) * 128],
                kbuf[:, h, s * 128:s * 128 + 256], start=True, stop=True))
            p.do("tensor", lambda e, s=s, h=h, b3=b3: e.matmul(
                ps_s4[b3][:, 256:272], qT[:, h, s * 128:(s + 1) * 128],
                kper[:, h, :], start=True, stop=True), inc="s_T")
            st_T[i] = p.c["s_T"]

        def emit_front_v(i):
            s, h = it_sh(i)
            t = mm * 4 + s
            mid = 0 if t == 0 else (1 if t == 1 else 2)
            b3 = i % 3
            b4 = i % 4
            p.wait("vector", "s_T", st_T[i])
            if m == 0 and i == 0:
                p.wait("vector", "c_const", c_const)
            p.chain("vector", "s_V",
                    lambda e, b3=b3, mid=mid: e.tensor_tensor(
                        ps_s4[b3][:, 0:272], ps_s4[b3][:, 0:272],
                        masks_sb[:, mid, :], ALU.add))
            p.chain("vector", "s_V",
                    lambda e, b3=b3, b4=b4: e.reduce_max(
                        negmax[b4][:], ps_s4[b3][:, 0:272], axis=AX.X,
                        negate=True))
            ma_V[i] = p.c["s_V"]

        def emit_exp(i):
            b3 = i % 3
            b4 = i % 4
            p.wait("scalar", "s_V", ma_V[i])
            if i >= 4:
                p.wait("scalar", "s_V", ts_V[i - 4])
            p.do("scalar", lambda e, b3=b3, b4=b4: e.activation(
                out=p_sb[b4][:], in_=ps_s4[b3][:, 0:272], func=AF.Exp,
                bias=negmax[b4][:], accum_out=rsum[b4][:]), inc="s_S")
            ex_S[i] = p.c["s_S"]

        def emit_softmax_norm(i):
            b4 = i % 4
            p.wait("vector", "s_S", ex_S[i])
            if i >= 4:
                p.wait("vector", "s_T", tr_T[i - 4])
            p.chain("vector", "s_V",
                    lambda e, b4=b4: e.reciprocal(rinv[b4][:], rsum[b4][:]))
            p.chain("vector", "s_V",
                    lambda e, b4=b4: e.tensor_scalar(
                        p_sb[b4][:], p_sb[b4][:], rinv[b4][:], None, ALU.mult))
            ts_V[i] = p.c["s_V"]

        def emit_back(i):
            s, h = it_sh(i)
            par = i % 2
            b4 = i % 4
            p.wait("tensor", "s_V", ts_V[i])
            if i >= 2:
                p.wait("tensor", "s_S", pe_S[i - 2])
            else:
                p.waitt("tensor", tr_free[par])
            p.do("tensor", lambda e, par=par, b4=b4: e.transpose(
                ps_tr2[par][:, 0:128], p_sb[b4][:, 0:128], id_bf[:]))
            p.do("tensor", lambda e, par=par, b4=b4: e.transpose(
                ps_tr2[par][:, 128:256], p_sb[b4][:, 128:256], id_bf[:]))
            p.do("tensor", lambda e, par=par, b4=b4: e.transpose(
                ps_tr2[par][0:16, 256:384], p_sb[b4][:, 256:272], id_bf[:]),
                inc="s_T")
            tr_T[i] = p.c["s_T"]
            p.wait("scalar", "s_T", tr_T[i])
            if i >= 2:
                p.wait("scalar", "s_T", av_T[i - 2])
            p.do("scalar", lambda e, par=par: e.copy(pT_sb[par][:, 0:384],
                                                     ps_tr2[par][:, 0:384]),
                 inc="s_S")
            pe_S[i] = p.c["s_S"]

        av_banks = [ps_o2[0], ev_ring.bufs[3]]

        def emit_av(i):
            s, h = it_sh(i)
            ab = i % 2
            p.wait("tensor", "s_S", pe_S[i])
            if i >= 2:
                p.waitt("tensor", ao_V[i - 2])
            elif i == 0:
                p.waitt("tensor", ao_free[0])
                p.wait("tensor", "s_S", gate_S)
            else:
                p.waitt("tensor", ev_ring.free[3])
            p.do("tensor", lambda e, s=s, h=h, ab=ab: e.matmul(
                av_banks[ab][:, 0:128], vbuf[:, s, h * 128:(h + 1) * 128],
                pT_sb[i % 2][:, 0:128], start=True, stop=False))
            p.do("tensor", lambda e, s=s, h=h, ab=ab: e.matmul(
                av_banks[ab][:, 0:128], vbuf[:, s + 1, h * 128:(h + 1) * 128],
                pT_sb[i % 2][:, 128:256], start=False, stop=False))
            p.do("tensor", lambda e, s=s, h=h, ab=ab: e.matmul(
                av_banks[ab][:, 0:128], vper[:, h * 128:(h + 1) * 128],
                pT_sb[i % 2][0:16, 256:384], start=False, stop=True), inc="s_T")
            av_T[i] = p.c["s_T"]

        def emit_ao(i):
            s, h = it_sh(i)
            p.wait("scalar", "s_T", av_T[i])
            p.do("scalar", lambda e, s=s, h=h, i=i: e.copy(
                aoT[:, h, s * 128:(s + 1) * 128], av_banks[i % 2][:, 0:128]),
                inc="s_S")
            ao_V[i] = ("s_S", p.c["s_S"])

        def emit_inserts(l):
            for (b, kk) in ins_sched[l]:
                jj = w1q_slot[b]
                if kk == 0:
                    p.waitt("tensor", R_w1.loaded[jj])
                    if g1_prev[0] is not None:
                        p.wait("tensor", "s_S", g1_prev[0])
                p.do("tensor", lambda e, jj=jj, kk=kk: e.matmul(
                    ps_o2[1][:, 0:512], R_w1.bufs[jj][:, kk, :],
                    xnT[:, kk, :], start=(kk == 0), stop=(kk == 15)),
                    inc="s_T")
                if kk == 15:
                    R_w1.free[jj] = ("s_T", p.c["s_T"])
                    if b < NBLK - 3:
                        w1q_slot[b + 3] = ring_load(
                            R_w1, w1_t[b + 3], cast_at["w1"][0][b + 3],
                            queues=("scalar",))
                    dst = (kvlatT[:, b, :] if b < 4 else hT[:, b, :])
                    p.wait("scalar", "s_T", p.c["s_T"])
                    p.do("scalar", lambda e, dst=dst, b=b: e.activation(
                        out=dst, in_=ps_o2[1][:, 0:512],
                        func=AF.Gelu_apprx_tanh, bias=b1T[:, b:b + 1]),
                        inc="s_S")
                    g1_prev[0] = p.c["s_S"]

        for j in range(3):
            score_free[j] = ev_ring.free[j]
        for l in range(NIT + 5):
            emit_inserts(l)
            if l < NIT:
                emit_scores(l)
            if 5 <= l:
                emit_ao(l - 5)
            if 1 <= l <= NIT:
                emit_front_v(l - 1)
            if 2 <= l <= NIT + 1:
                emit_softmax_norm(l - 2)
            if 3 <= l <= NIT + 2:
                emit_back(l - 3)
            if 4 <= l <= NIT + 3:
                emit_av(l - 4)
            if 1 <= l <= NIT:
                emit_exp(l - 1)
        core_last_T = p.c["s_T"]
        core_last_S = p.c["s_S"]
        tr_free = [("s_S", pe_S[NIT - 2]), ("s_S", pe_S[NIT - 1])]
        ao_free = [ao_V[NIT - 2], ao_V[NIT - 1]]
        for j in range(3):
            last_i = max(i for i in range(NIT) if i % 3 == j)
            ev_ring.free[j] = ("s_S", ex_S[last_i])
        ev_ring.free[3] = ao_V[NIT - 1]

        for sl in range(8):
            jj = ring_load(R_wo, wo_t[sl], cast_at["wo"][sl], queues=qs2)
            p.waitt("tensor", R_wo.loaded[jj])
            for s in range(NSUB):
                bk = ev_next()
                if sl == 0 and s == 0:
                    p.wait("tensor", "s_S", core_last_S)
                    p.waitt("vector", acc_base_S)
                for c in range(16):
                    p.do("tensor", lambda e, jj=jj, c=c, s=s, bk=bk: e.matmul(
                        ev_ring.bufs[bk][:, 0:256],
                        aoT[:, c, s * 128:(s + 1) * 128],
                        R_wo.bufs[jj][:, c, :], start=(c == 0), stop=(c == 15)),
                        inc=("s_T" if c == 15 else None))
                g_T = p.c["s_T"]
                p.wait("vector", "s_T", g_T)
                p.chain("vector", "s_V",
                        lambda e, sl=sl, s=s, bk=bk: e.tensor_tensor(
                            acc[:, s, sl * 256:(sl + 1) * 256],
                            ev_ring.bufs[bk][:, 0:256],
                            acc[:, s, sl * 256:(sl + 1) * 256], ALU.add))
                ev_ring.free[bk] = ("s_V", p.c["s_V"])
            R_wo.free[jj] = ("s_T", p.c["s_T"])
        proj_last_T = p.c["s_T"]
        p.waitt("vector", acc_base_S)
        for s in range(NSUB):
            p.chain("vector", "s_V",
                    lambda e, s=s: e.scalar_tensor_tensor(
                        acc[:, s, :], b2bc[:], gcol[:, s:s + 1], acc[:, s, :],
                        ALU.mult, ALU.add))
        proj_sub_V = [p.c["s_V"]] * NSUB

        def emit_xside_prerun():
            nonlocal xin_side_next, ln_carry
            nb = (m + 1) // MPB
            nrow = ((m + 1) % MPB) * MEGA
            p.wait("scalar", "s_V", xside_consumed_V)
            p.wait("scalar", "s_S", core_last_S)
            p.wait("scalar", "s_T", core_last_T)
            if ins_on:
                p.wait("scalar", "s_T", pass2_T)
            p.dma("scalar", xside[:], x_d[nb, nrow:nrow + 128, :], "xs")
            xin_side_next = ("xs", p.c["xs"])

            p.waitt("vector", xin_side_next)
            p.wait("vector", "s_T", proj_last_T)
            p.waitt("vector", acc_base_S)
            p.chain("vector", "s_V",
                    lambda e: e.reduce_sum(sumx[:, 0:1], xside[:], axis=AX.X))
            pre_red = p.c["s_V"]
            p.waitt("scalar", xin_side_next)
            p.wait("scalar", "s_T", proj_last_T)
            p.waitt("scalar", lnT_done)
            p.do("scalar", lambda e: e.activation(
                out=aoT[:, 0:4, :], in_=xside[:], func=AF.Square,
                accum_out=sumsq[:, 0:1]), inc="s_S")
            pre_sq = p.c["s_S"]
            p.wait("vector", "s_S", pre_sq)
            p.chain("vector", "s_V",
                    lambda e: e.tensor_scalar(meanv[:, 0:1], sumx[:, 0:1],
                                              INV_D, None, ALU.mult))
            p.chain("vector", "s_V",
                    lambda e: e.tensor_scalar(varv[:, 0:1], sumsq[:, 0:1],
                                              INV_D, EPS, ALU.mult, ALU.add))
            p.chain("vector", "s_V",
                    lambda e: e.tensor_tensor(s1c[:, 0:1], meanv[:, 0:1],
                                              meanv[:, 0:1], ALU.mult))
            p.chain("vector", "s_V",
                    lambda e: e.tensor_tensor(varv[:, 0:1], varv[:, 0:1],
                                              s1c[:, 0:1], ALU.subtract))
            pre_mathA = p.c["s_V"]
            p.wait("scalar", "s_V", pre_mathA)
            p.do("scalar", lambda e: e.activation(
                out=stdv[:, 0:1], in_=varv[:, 0:1], func=AF.Sqrt), inc="s_S")
            pre_std = p.c["s_S"]
            p.wait("vector", "s_S", pre_std)
            p.chain("vector", "s_V",
                    lambda e: e.reciprocal(rstd[:, 0:1], stdv[:, 0:1]))
            p.chain("vector", "s_V",
                    lambda e: e.tensor_scalar(s0c[:, 0:1], rstd[:, 0:1], 1.0,
                                              None, ALU.add))
            p.chain("vector", "s_V",
                    lambda e: e.tensor_tensor(s1c[:, 0:1], meanv[:, 0:1],
                                              rstd[:, 0:1], ALU.mult))
            p.chain("vector", "s_V",
                    lambda e: e.tensor_scalar(s1c[:, 0:1], s1c[:, 0:1], -1.0,
                                              None, ALU.mult))
            pre_mathB = p.c["s_V"]
            p.chain("vector", "s_V",
                    lambda e: e.tensor_scalar(aoT[:, 0:4, :], xside[:],
                                              rstd[:, 0:1], s1c[:, 0:1],
                                              ALU.mult, ALU.add))
            ln_carry = (pre_red, pre_sq, pre_mathA, pre_std, pre_mathB,
                        ("s_V", p.c["s_V"]))

        acc_final_V = [0] * NSUB
        pass2_T = 0
        for q in range(4):
            jstart = NBLK if (q == 0 and ins_on) else 0
            for j in range(jstart, 16):
                jq = q * 16 + j
                jj = ring_load(R_w1, w1_t[jq], cast_at["w1"][q][j], queues=qs2)
                bk = ev_next()
                p.waitt("tensor", R_w1.loaded[jj])
                for c in range(16):
                    p.do("tensor", lambda e, jj=jj, c=c, bk=bk: e.matmul(
                        ev_ring.bufs[bk][:], R_w1.bufs[jj][:, c, :], xnT[:, c, :],
                        start=(c == 0), stop=(c == 15)),
                        inc=("s_T" if c == 15 else None))
                g_T = p.c["s_T"]
                R_w1.free[jj] = ("s_T", g_T)
                p.wait("scalar", "s_T", g_T)
                if j == jstart and q > 0:
                    p.wait("scalar", "s_T", pass2_T)
                if j == jstart and q == 0:
                    p.wait("scalar", "s_T", core_last_T)
                    p.waitt("scalar", lnT_done)
                p.do("scalar", lambda e, j=j, jq=jq, bk=bk: e.activation(
                    out=hT[:, j, :], in_=ev_ring.bufs[bk][:],
                    func=AF.Gelu_apprx_tanh, bias=b1T[:, jq:jq + 1]), inc="s_S")
                ev_ring.free[bk] = ("s_S", p.c["s_S"])
            hq_S = p.c["s_S"]
            for sl in range(8):
                jj = ring_load(R_w2, w2_t[q, sl], cast_at["w2"][q][sl],
                               queues=("sync", "scalar"))
                p.waitt("tensor", R_w2.loaded[jj])
                for s in range(NSUB):
                    bk = ev_next()
                    if sl == 0 and s == 0:
                        p.wait("tensor", "s_S", hq_S)
                    for c in range(16):
                        if q == 0 and ins_on and c < 4:
                            hop = kvlatT[:, c, s * 128:(s + 1) * 128]
                        else:
                            hop = hT[:, c, s * 128:(s + 1) * 128]
                        p.do("tensor",
                             lambda e, jj=jj, c=c, s=s, bk=bk, hop=hop:
                             e.matmul(
                                 ev_ring.bufs[bk][:, 0:256],
                                 hop, R_w2.bufs[jj][:, c, :],
                                 start=(c == 0), stop=(c == 15)),
                             inc=("s_T" if c == 15 else None))
                    g_T = p.c["s_T"]
                    p.wait("vector", "s_T", g_T)
                    if q == 0 and sl == 0:
                        p.wait("vector", "s_V", max(gcol_V, proj_sub_V[s]))
                    p.chain("vector", "s_V",
                            lambda e, sl=sl, s=s, bk=bk: e.scalar_tensor_tensor(
                                acc[:, s, sl * 256:(sl + 1) * 256],
                                ev_ring.bufs[bk][:, 0:256], gcol[:, s:s + 1],
                                acc[:, s, sl * 256:(sl + 1) * 256],
                                ALU.mult, ALU.add))
                    ev_ring.free[bk] = ("s_V", p.c["s_V"])
                    if q == 3:
                        acc_final_V[s] = p.c["s_V"]
                R_w2.free[jj] = ("s_T", p.c["s_T"])
            pass2_T = p.c["s_T"]
            if q == 0 and m + 1 < nmega:
                emit_xside_prerun()
        gcol_read_V = p.c["s_V"]
        hT_last_reader_T = p.c["s_T"]
        xnT_last_reader_T = p.c["s_T"]

        for s in range(NSUB):
            p.wait("scalar", "s_V", acc_final_V[s])
            sn = "o%d" % s
            p.dma("scalar", y_d[b, row0 + s * 128:row0 + (s + 1) * 128, :],
                  acc[:, s, :], sn)
            acc_out_at[s] = ("o%d" % s, p.c[sn])

        if m + 1 < nmega:
            nb = (m + 1) // MPB
            nrow = ((m + 1) % MPB) * MEGA
            for s in range(1, NSUB):
                p.waitt("scalar", acc_out_at[s])
                sn = "xi%d" % s
                p.dma("scalar", acc[:, s, :],
                      x_d[nb, nrow + s * 128:nrow + (s + 1) * 128, :], sn)
                xin_sub[s] = (sn, p.c[sn])
            xin_sub[0] = xin_side_next

    for s in range(NSUB):
        p.wait("sync", "o%d" % s, p.c["o%d" % s])

    with nc.Block() as block:
        @block.tensor
        def _(e):
            for f in p.ops["tensor"]:
                f(e)

        @block.vector
        def _(e):
            for f in p.ops["vector"]:
                f(e)

        @block.scalar
        def _(e):
            for f in p.ops["scalar"]:
                f(e)

        @block.sync
        def _(e):
            for f in p.ops["sync"]:
                f(e)

        @block.gpsimd
        def _(e):
            for f in p.ops["gpsimd"]:
                f(e)

    return nc


_NC_CACHE = {}


def kernel(L0_attn_x, L0_attn_Wq, L0_attn_Wkv, L0_attn_Wk, L0_attn_Wv, L0_attn_Wo,
           L0_ffn_router_weight, L0_ffn_router_bias, L0_ffn_w1, L0_ffn_b1,
           L0_ffn_w2, L0_ffn_b2):
    x = np.ascontiguousarray(np.asarray(L0_attn_x, dtype=np.float32))
    weights = {
        "wq": np.ascontiguousarray(np.asarray(L0_attn_Wq, dtype=np.float32)),
        "wkv": np.ascontiguousarray(np.asarray(L0_attn_Wkv, dtype=np.float32)),
        "wk": np.ascontiguousarray(np.asarray(L0_attn_Wk, dtype=np.float32)),
        "wv": np.ascontiguousarray(np.asarray(L0_attn_Wv, dtype=np.float32)),
        "wo": np.ascontiguousarray(np.asarray(L0_attn_Wo, dtype=np.float32)),
        "rw": np.ascontiguousarray(np.asarray(L0_ffn_router_weight, dtype=np.float32)),
        "rb": np.ascontiguousarray(np.asarray(L0_ffn_router_bias, dtype=np.float32)),
        "w1": np.ascontiguousarray(np.asarray(L0_ffn_w1, dtype=np.float32)),
        "b1": np.ascontiguousarray(np.asarray(L0_ffn_b1, dtype=np.float32)),
        "w2": np.ascontiguousarray(np.asarray(L0_ffn_w2, dtype=np.float32)),
        "b2": np.ascontiguousarray(np.asarray(L0_ffn_b2, dtype=np.float32)),
    }
    if "nc" not in _NC_CACHE:
        _NC_CACHE["nc"] = build()
    nc = _NC_CACHE["nc"]
    in_maps = []
    for c in range(NCORES):
        d = {"x": x[c * BPC:(c + 1) * BPC]}
        d.update(weights)
        in_maps.append(d)
    res = run_bass_kernel_spmd(nc, in_maps, core_ids=list(range(NCORES)))
    return np.concatenate([r["y"] for r in res.results], axis=0)


if __name__ == "__main__":
    rng = np.random.default_rng(0)
    ins = {
        "L0_attn_x": rng.standard_normal((16, S, Dm), dtype=np.float32),
        "L0_attn_Wq": rng.standard_normal((Dm, Dm), dtype=np.float32) / np.sqrt(Dm),
        "L0_attn_Wkv": rng.standard_normal((Dm, L), dtype=np.float32) / np.sqrt(Dm),
        "L0_attn_Wk": rng.standard_normal((L, Dm), dtype=np.float32) / np.sqrt(L),
        "L0_attn_Wv": rng.standard_normal((L, Dm), dtype=np.float32) / np.sqrt(L),
        "L0_attn_Wo": rng.standard_normal((Dm, Dm), dtype=np.float32) / np.sqrt(Dm),
        "L0_ffn_router_weight": rng.standard_normal(Dm).astype(np.float32) * 0.02,
        "L0_ffn_router_bias": rng.standard_normal(1).astype(np.float32) * 0.1,
        "L0_ffn_w1": rng.standard_normal((Dm, DFF), dtype=np.float32) / np.sqrt(Dm),
        "L0_ffn_b1": rng.standard_normal(DFF).astype(np.float32) * 0.1,
        "L0_ffn_w2": rng.standard_normal((DFF, Dm), dtype=np.float32) / np.sqrt(DFF),
        "L0_ffn_b2": rng.standard_normal(Dm).astype(np.float32) * 0.1,
    }
    out = kernel(**ins)
    print(out.shape, out.dtype, np.abs(out).mean())
